# version 41
# baseline (speedup 1.0000x reference)
import contextlib
import numpy as np

import concourse.bass as bass
import concourse.mybir as mybir
from concourse.bass_utils import run_bass_kernel_spmd

F32 = mybir.dt.float32
BF16 = mybir.dt.bfloat16
AF = mybir.ActivationFunctionType
ALU = mybir.AluOpType
AX = mybir.AxisListType

NCORES = 8
C = 1024
KC = 8
TC = 2048
HALF = 1024
WARM = 512
NE = 8
HID = 2048
LN_EPS = 1e-5
NEG = -1e30
NT = HALF // 512

ENG = ("tensor", "vector", "scalar", "gpsimd", "sync")


class Buf:
    __slots__ = ("name", "w", "r")

    def __init__(self, name):
        self.name = name
        self.w = None
        self.r = {}


class Prog:

    def __init__(self, nc, sems):
        self.nc = nc
        self.sems = sems
        self.count = {k: 0 for k in sems}
        self.sat = {e: {} for e in ENG}
        self.ops = {e: [] for e in ENG}
        self.engsem = {"tensor": "ePE", "vector": "eDVE", "scalar": "eACT",
                       "gpsimd": "ePOOL", "sync": "eSP"}
        self.dmalanes = {"gpsimd": ["dGP0", "dGP1", "dGP2", "dGP3",
                                    "dGP4", "dGP5"],
                         "sync": ["dSP0", "dSP1", "dSP2"],
                         "scalar": ["dACT0"]}
        self.dmaidx = {k: 0 for k in self.dmalanes}

    def op(self, eng, outs, ins, fn, dma=False):
        waits = {}

        def need(m):
            if m is not None:
                s, v = m
                if v > waits.get(s, 0):
                    waits[s] = v

        for b in ins:
            need(b.w)
        for b in outs:
            need(b.w)
            for s, v in b.r.items():
                need((s, v))
        if dma:
            lanes = self.dmalanes[eng]
            cs = lanes[self.dmaidx[eng] % len(lanes)]
            self.dmaidx[eng] += 1
            inc = 16
            if self.count[cs] > 0:
                need((cs, self.count[cs]))
        else:
            cs = self.engsem[eng]
            inc = 1
        wl = []
        for s, v in sorted(waits.items()):
            if self.sat[eng].get(s, 0) >= v:
                continue
            if eng == "tensor" and s == "ePE":
                continue
            wl.append((s, v))
            self.sat[eng][s] = v
        self.count[cs] += inc
        v = self.count[cs]
        self.ops[eng].append((wl, fn, (cs, inc)))
        for b in outs:
            b.w = (cs, v)
            b.r = {}
        for b in ins:
            if b.r.get(cs, 0) < v:
                b.r[cs] = v
        return (cs, v)

    def barrier_wait(self, eng, milestones):
        wl = []
        for s, v in milestones:
            if self.sat[eng].get(s, 0) >= v:
                continue
            wl.append((s, v))
            self.sat[eng][s] = v
        if wl:
            self.ops[eng].append((wl, None, None))

    def check_deadlock(self):
        heads = {e: 0 for e in ENG}
        vals = {k: 0 for k in self.sems}
        total = sum(len(v) for v in self.ops.values())
        done = 0
        progress = True
        while progress:
            progress = False
            for e in ENG:
                while heads[e] < len(self.ops[e]):
                    wl, fn, incs = self.ops[e][heads[e]]
                    if any(vals[s] < v for s, v in wl):
                        break
                    if incs is not None:
                        vals[incs[0]] += incs[1]
                    heads[e] += 1
                    done += 1
                    progress = True
        if done != total:
            detail = []
            for e in ENG:
                if heads[e] < len(self.ops[e]):
                    wl = self.ops[e][heads[e]][0]
                    blocked = [(s, v, vals[s]) for s, v in wl if vals[s] < v]
                    detail.append(f"{e}@{heads[e]}/{len(self.ops[e])} waits {blocked}")
            raise RuntimeError("deadlock in emitted program: " + "; ".join(detail))

    def emit(self, eng_obj, eng):
        for wl, fn, incs in self.ops[eng]:
            for s, v in wl:
                eng_obj.wait_ge(self.sems[s], v)
            if fn is not None:
                inst = fn()
                if incs is not None:
                    inst.then_inc(self.sems[incs[0]], incs[1])


def col_bcast(t, ci, n, free):
    return bass.AP(t, ci, [[free, 128], [0, n]])


def build_nc():
    nc = bass.Bass("TRN2", target_bir_lowering=False)

    xT = nc.dram_tensor("xT", [C, WARM + 1 + TC], F32, kind="ExternalInput").ap()
    wkT = nc.dram_tensor("wkT", [C, C], F32, kind="ExternalInput").ap()
    wvT = nc.dram_tensor("wvT", [C, C], F32, kind="ExternalInput").ap()
    wrT = nc.dram_tensor("wrT", [C, C], F32, kind="ExternalInput").ap()
    woT = nc.dram_tensor("woT", [C, C], F32, kind="ExternalInput").ap()
    wuT = nc.dram_tensor("wuT", [NE, C, HID], F32, kind="ExternalInput").ap()
    wdT = nc.dram_tensor("wdT", [NE, HID, C], F32, kind="ExternalInput").ap()
    cst_d = nc.dram_tensor("cst", [128, 408], F32, kind="ExternalInput").ap()
    cstb_d = nc.dram_tensor("cstb", [128, 1028], F32, kind="ExternalInput").ap()
    out_d = nc.dram_tensor("out", [C, TC], F32, kind="ExternalOutput").ap()

    xT_r = xT.rearrange("(kc p) t -> p kc t", p=128)
    out_r = out_d.rearrange("(kc p) t -> p kc t", p=128)

    st = contextlib.ExitStack()
    with st:
        def sb(name, shape, dt):
            return st.enter_context(nc.sbuf_tensor(name, shape, dt))

        X = sb("X", [128, KC * HALF], F32)
        A2 = sb("A2", [128, KC * HALF], F32)
        D1 = sb("D1", [128, HALF], F32)
        DK0 = sb("DK0", [128, HALF], F32)
        DK1 = sb("DK1", [128, HALF], F32)
        B1 = sb("B1", [128, KC * HALF], BF16)
        B2 = sb("B2", [128, KC * HALF], BF16)
        BR = sb("BR", [128, KC * HALF], BF16)
        KV0 = sb("KV0", [128, HALF + 8], BF16)
        KV1 = sb("KV1", [128, HALF + 8], BF16)
        Y1B = sb("Y1B", [128, KC * HALF], BF16)
        W0 = sb("W0", [128, KC * 1024], BF16)
        W1 = sb("W1", [128, KC * 1024], BF16)
        W2 = sb("W2", [128, KC * 1024], BF16)
        WBt = sb("WBt", [128, HALF], BF16)
        TR0 = sb("TR0", [128, 1024], BF16)
        TR1 = sb("TR1", [128, 1024], BF16)
        CONST = sb("CONST", [128, 408], F32)
        CONSTB = sb("CONSTB", [128, 1028], BF16)
        VEC = CONST[:, 0:80]
        DEC = sb("DEC", [128, KC], F32)
        GT = CONST[:, 80:144]
        EB = CONST[0:8, 272:273]
        IDT = CONST[:, 144:272]
        OHB = CONSTB[0:8, 1:1025]
        ONR = CONST[0:1, 273:401]
        ONCB = CONSTB[:, 0:1]
        STATE = sb("STATE", [128, KC], F32)
        SSUM, SSSQ, STMP = D1, DK0, DK1
        EPSC = sb("EPSC", [1, 1], F32)
        NHC = sb("NHC", [1, 1], F32)
        MR1 = sb("MR1", [128, 8], F32)
        TB2 = sb("TB2", [128, 8], F32)
        TB3 = sb("TB3", [128, 8], F32)

        U = [st.enter_context(nc.psum_tensor(f"U{i}", [128, 1024], F32))
             for i in range(4)]

        sems = {}
        for s in ("ePE", "eDVE", "eACT", "ePOOL", "eSP",
                  "dGP0", "dGP1", "dGP2", "dGP3", "dGP4", "dGP5",
                  "dSP0", "dSP1", "dSP2", "dACT0"):
            sems[s] = st.enter_context(nc.semaphore(s))

        P = Prog(nc, sems)

        bA2, bD1 = Buf("A2"), Buf("D1")
        bXc = [Buf(f"X.{i}") for i in range(KC)]
        bDK = [Buf("DK0"), Buf("DK1")]
        bB1c = [Buf(f"B1.{i}") for i in range(KC)]
        bB2c = [Buf(f"B2.{i}") for i in range(KC)]
        bBRc = [Buf(f"BR.{i}") for i in range(KC)]
        bKV = [Buf("KV0"), Buf("KV1")]
        bY1c = [Buf(f"Y1B.{i}") for i in range(KC)]
        bWB = Buf("WB")
        bW = [Buf("W0"), Buf("W1"), Buf("W2")]
        bTR = [Buf("TR0"), Buf("TR1")]
        bCONST, bDEC, bCONSTB = Buf("CONST"), Buf("DEC"), Buf("CONSTB")
        bVEC = bGT = bEB = bIDT = bONR = bCONST
        bOHB = bONCB = bCONSTB
        bSTATE = Buf("STATE")
        bSSUM, bSSSQ, bSTMP = bD1, bDK[0], bDK[1]
        bMR1, bTB2, bTB3 = Buf("MR1"), Buf("TB2"), Buf("TB3")
        bU = [Buf(f"U{i}") for i in range(4)]
        bOUT = Buf("OUT")
        WSLOT = [W0, W1, W2]
        TRS = [TR0, TR1]
        DKS = [DK0, DK1]
        KVS = [KV0, KV1]

        V, S, T, G, SY = nc.vector, nc.scalar, nc.tensor, nc.gpsimd, nc.sync

        Xv = X[:, :].rearrange("p (c t) -> p c t", c=KC)
        A2v = A2[:, :].rearrange("p (c t) -> p c t", c=KC)
        B1v = B1[:, :].rearrange("p (c t) -> p c t", c=KC)
        B2v = B2[:, :].rearrange("p (c t) -> p c t", c=KC)
        BRv = BR[:, :].rearrange("p (c t) -> p c t", c=KC)
        Y1v = Y1B[:, :].rearrange("p (c t) -> p c t", c=KC)
        Wv_ = [w[:, :].rearrange("p (c m) -> p c m", c=KC) for w in WSLOT]

        vslice = {name: VEC[:, i * KC:(i + 1) * KC]
                  for i, name in enumerate(
                      ["tmk", "tmv", "tmr", "td", "tf", "lng", "lnb",
                       "tmk1", "tmv1", "tmr1"])}

        P.op("sync", [bCONST], [],
             lambda: SY.dma_start(CONST[:, :], cst_d), dma=True)
        P.op("gpsimd", [bCONSTB], [],
             lambda: G.dma_start(CONSTB[:, :], cstb_d), dma=True)
        P.op("scalar", [bDEC], [bVEC],
             lambda: S.activation(DEC[:, :], vslice["td"], AF.Exp))
        P.op("vector", [bMR1], [], lambda: V.memset(MR1[:, :], NEG))

        bEPSC = Buf("EPSC")
        P.op("vector", [bEPSC], [], lambda: V.memset(EPSC[:, :], LN_EPS))
        bNHC = Buf("NHC")
        P.op("vector", [bNHC], [], lambda: V.memset(NHC[:, :], -0.5))
        for a, b_ in (("tmk1", "tmk"), ("tmv1", "tmv"), ("tmr1", "tmr")):
            P.op("vector", [bVEC], [bVEC],
                 lambda a2=a, b2=b_: V.tensor_single_scalar(
                     vslice[a2], vslice[b2], 1.0, ALU.subtract))

        ring = {"i": 0}

        def stream_weight(src_ap):
            i = ring["i"] % 3
            ring["i"] += 1
            slot, bslot = WSLOT[i], bW[i]
            src = src_ap.rearrange("(kc p) m -> p kc m", p=128)
            P.op("gpsimd", [bslot], [],
                 lambda sl=slot, sr=src: G.dma_start(
                     sl[:, :].rearrange("p (c m) -> p c m", c=KC), sr), dma=True)
            return i

        def mm(ui, nt, lhsT, rhs, lhs_buf, rhs_buf, start, stop):
            M = lhsT.shape[-1]
            P.op("tensor", [bU[ui]], [lhs_buf, rhs_buf],
                 lambda u=ui, n=nt, l=lhsT, r=rhs, m_=M, st_=start, sp=stop:
                 T.matmul(U[u][0:m_, n * 512:n * 512 + r.shape[-1]], l, r,
                          start=st_, stop=sp))

        def proj_mms(ws, rhs_view, rhs_bufs, mc, pair, w=HALF):
            for nt in range(w // 512):
                for kc in range(KC):
                    mm(pair, nt, Wv_[ws][:, kc, mc * 128:(mc + 1) * 128],
                       rhs_view[:, kc, nt * 512:(nt + 1) * 512],
                       bW[ws], rhs_bufs[kc], start=(kc == 0), stop=(kc == KC - 1))

        def ln_copy(src_view, src_buf, bft, bfbs, mc, act=False):
            bfv = bft[:, :].rearrange("p (c t) -> p c t", c=KC)
            if act:
                P.op("scalar", [bfbs[mc]], [src_buf],
                     lambda m=mc, sv=src_view, bv=bfv: S.activation(
                         bv[:, m, :], sv[:, m, 0:HALF], AF.Copy))
            else:
                P.op("vector", [bfbs[mc]], [src_buf],
                     lambda m=mc, sv=src_view, bv=bfv: V.tensor_copy(
                         bv[:, m, :], sv[:, m, 0:HALF]))

        def ln_sq(bft, bfbs, sqt, sqbs, mc, dve=False):
            bfv = bft[:, :].rearrange("p (c t) -> p c t", c=KC)
            sqv = sqt[:, :].rearrange("p (c t) -> p c t", c=KC)
            if dve:
                P.op("vector", [sqbs[mc]], [bfbs[mc]],
                     lambda m=mc, bv=bfv, sv2=sqv: V.tensor_tensor(
                         sv2[:, m, :], bv[:, m, :], bv[:, m, :], ALU.mult))
            else:
                P.op("scalar", [sqbs[mc]], [bfbs[mc]],
                     lambda m=mc, bv=bfv, sv2=sqv: S.activation(
                         sv2[:, m, :], bv[:, m, :], AF.Square))

        def ln_copy_sq(src_view, src_buf, bft, bfbs, sqt, sqbs, mc):
            ln_copy(src_view, src_buf, bft, bfbs, mc)
            ln_sq(bft, bfbs, sqt, sqbs, mc)

        def ln_reduce(bft, bfbs, sqt, sqbs):
            bfv = bft[:, :].rearrange("p (c t) -> p c t", c=KC)
            sqv = sqt[:, :].rearrange("p (c t) -> p c t", c=KC)
            for nt in range(NT):
                for kc in range(KC):
                    mm(2, nt, ONCB[:, 0:1], bfv[:, kc, nt * 512:(nt + 1) * 512],
                       bONCB, bfbs[kc], start=(kc == 0), stop=(kc == KC - 1))
                for kc in range(KC):
                    mm(3, nt, ONCB[:, 0:1], sqv[:, kc, nt * 512:(nt + 1) * 512],
                       bONCB, sqbs[kc], start=(kc == 0), stop=(kc == KC - 1))
            P.op("vector", [bSSUM], [bU[2]],
                 lambda: V.tensor_copy(SSUM[0:1, :], U[2][0:1, 0:HALF]))
            P.op("vector", [bSSSQ], [bU[3]],
                 lambda: V.tensor_copy(SSSQ[0:1, :], U[3][0:1, 0:HALF]))
            inv = 1.0 / C
            P.op("vector", [bSSUM], [bSSUM],
                 lambda: V.tensor_single_scalar(SSUM[0:1, :], SSUM[0:1, :], inv,
                                                ALU.mult))
            P.op("vector", [bSTMP], [bSSUM],
                 lambda: V.tensor_mul(STMP[0:1, :], SSUM[0:1, :], SSUM[0:1, :]))
            P.op("vector", [bSSSQ], [bSSSQ, bSTMP],
                 lambda: V.scalar_tensor_tensor(SSSQ[0:1, :], SSSQ[0:1, :], inv,
                                                STMP[0:1, :], ALU.mult,
                                                ALU.subtract))
            P.op("scalar", [bSTMP], [bSSSQ, bEPSC],
                 lambda: S.activation(STMP[0:1, :], SSSQ[0:1, :], AF.Ln,
                                      bias=EPSC[0:1, 0:1]))
            P.op("scalar", [bSTMP], [bSTMP, bNHC],
                 lambda: S.activation(STMP[0:1, :], STMP[0:1, :], AF.Exp,
                                      scale=NHC[0:1, 0:1]))
            for nt in range(NT):
                mm(2, nt, ONR[0:1, :], SSUM[0:1, nt * 512:(nt + 1) * 512],
                   bONR, bSSUM, start=True, stop=True)
                mm(3, nt, ONR[0:1, :], STMP[0:1, nt * 512:(nt + 1) * 512],
                   bONR, bSTMP, start=True, stop=True)

        mix_cfg = {"c0": 0}

        def mixes(targets, w=HALF):
            c0 = mix_cfg["c0"]
            for ci in range(KC):
                pool = ci % 2 == 1
                d, bd = (DK0, bDK[0]) if pool else (D1, bD1)
                xs, bxs = KVS[ci % 2], bKV[ci % 2]
                P.op("gpsimd", [bxs], [],
                     lambda c=ci, x2=xs, w2=w, cc=c0: G.dma_start(
                         x2[:, 0:w2 + 1], xT_r[:, c, cc:cc + 1 + w2]), dma=True)
                P.op("vector", [bd], [bxs],
                     lambda d2=d, x2=xs, w2=w: V.tensor_sub(
                         d2[:, 0:w2], x2[:, 1:w2 + 1], x2[:, 0:w2]))
                for dv, dbufs, tm in targets:
                    P.op("vector", [dbufs[ci]], [bd, bxs, bVEC],
                         lambda c=ci, d2=d, x2=xs, dv2=dv, t_=tm, w2=w:
                         V.scalar_tensor_tensor(
                             dv2[:, c, 0:w2], d2[:, 0:w2], vslice[t_][:, c:c + 1],
                             x2[:, 1:w2 + 1], ALU.mult, ALU.add))

        def attn_prologue(x_col0, first, w=HALF, with_r=True):
            mix_cfg["c0"] = x_col0
            tg = [(B1v, bB1c, "tmk1"), (B2v, bB2c, "tmv1")]
            if with_r:
                tg.append((BRv, bBRc, "tmr1"))
            mixes(tg, w)

        def attn_phase(x_col0, kind, first, slots, prologue_done=False):
            warm = (kind == "warm")
            if not prologue_done:
                attn_prologue(x_col0, first, w=WARM if warm else HALF,
                              with_r=not warm)
            if not warm:
                for g2 in range(2):
                    P.op("sync", [bXc[g2 * 4 + i] for i in range(4)], [],
                         lambda c0=x_col0, g3=g2: SY.dma_start(
                             Xv[:, g3 * 4:(g3 + 1) * 4, :],
                             xT_r[:, g3 * 4:(g3 + 1) * 4,
                                  c0 + 1:c0 + 1 + HALF]), dma=True)

            w = WARM if warm else HALF
            for mc in range(KC):
                dk, bdk = DKS[mc % 2], bDK[mc % 2]
                kvt, bkv = KVS[mc % 2], bKV[mc % 2]
                uk = (mc % 2) * 2 if warm else 0
                uv = uk + 1
                proj_mms(slots["wk"], B1v, bB1c, mc, pair=uk, w=w)
                proj_mms(slots["wv"], B2v, bB2c, mc, pair=uv, w=w)
                tr, btr = TRS[mc % 2], bTR[mc % 2]
                P.op("scalar", [bdk], [bU[uk]],
                     lambda d=dk, w2=w, u=uk: S.activation(
                         d[:, 0:w2], U[u][:, 0:w2], AF.Copy))
                P.op("scalar", [btr], [bU[uv]],
                     lambda t2=tr, w2=w, u=uv: S.activation(
                         t2[:, 0:w2], U[u][:, 0:w2], AF.Copy))
                P.op("gpsimd", [bkv], [btr, bdk],
                     lambda d=dk, t2=tr, kv=kvt, w2=w: G.tensor_tensor(
                         kv[:, 0:w2], t2[:, 0:w2], d[:, 0:w2], ALU.mult))
                init = 0.0 if (warm and first) else STATE[:, mc:mc + 1]
                ins_l = [bkv, bDEC] + ([] if (warm and first) else [bSTATE])
                P.op("vector", [bA2], ins_l,
                     lambda m=mc, kv=kvt, i=init, w2=w: V.tensor_tensor_scan(
                         A2v[:, m, 0:w2], col_bcast(DEC, m, w2, KC), kv[:, 0:w2],
                         i, ALU.mult, ALU.add))
                P.op("vector", [bSTATE], [bA2],
                     lambda m=mc, w2=w: V.tensor_copy(
                         STATE[:, m:m + 1],
                         A2[:, m * HALF + w2 - 1:m * HALF + w2]))
                if not warm:
                    P.op("vector", [bA2], [bA2, bkv, bVEC],
                         lambda m=mc, kv=kvt: V.scalar_tensor_tensor(
                             A2v[:, m, :], kv[:, 0:HALF], vslice["tf"][:, m:m + 1],
                             A2v[:, m, :], ALU.mult, ALU.add))
            if warm:
                return
            s_wo = stream_weight(woT)
            pre = {"u0": stream_weight(wuT[0, :, 0:1024])}

            for mc in range(KC):
                pr = mc % 2
                proj_mms(slots["wr"], BRv, bBRc, mc, pair=pr)
                P.op("vector", [bB1c[mc]], [bU[pr]],
                     lambda m=mc, p_=pr: V.tensor_copy(B1v[:, m, :],
                                                       U[p_][:, 0:HALF]))
                P.op("vector", [bA2], [bA2, bB1c[mc]],
                     lambda m=mc: V.tensor_tensor(A2v[:, m, :], A2v[:, m, :],
                                                  B1v[:, m, :], ALU.mult))
                ln_copy(A2v, bA2, B2, bB2c, mc, act=True)
            pre["d0"] = stream_weight(wdT[0, 0:1024, :])
            for mc in range(KC):
                ln_sq(B2, bB2c, BR, bBRc, mc, dve=(mc < 4))
            ln_reduce(B2, bB2c, BR, bBRc)

            P.op("vector", [bTR[0]], [bU[2]],
                 lambda: V.tensor_copy(TR0[:, :], U[2][:, 0:HALF]))
            P.op("vector", [bTR[1]], [bU[3]],
                 lambda: V.tensor_copy(TR1[:, :], U[3][:, 0:HALF]))
            for mc in range(KC):
                P.op("vector", [bB2c[mc]], [bB2c[mc], bTR[0]],
                     lambda m=mc: V.tensor_tensor(B2v[:, m, :], B2v[:, m, :],
                                                  TR0[:, :], ALU.subtract))
                P.op("vector", [bB2c[mc]], [bB2c[mc], bTR[1]],
                     lambda m=mc: V.tensor_tensor(B2v[:, m, :], B2v[:, m, :],
                                                  TR1[:, :], ALU.mult))
                P.op("vector", [bB1c[mc]], [bB2c[mc], bVEC],
                     lambda m=mc: V.tensor_scalar(
                         B1v[:, m, :], B2v[:, m, :], vslice["lng"][:, m:m + 1],
                         vslice["lnb"][:, m:m + 1], ALU.mult, ALU.add))
            for mc in range(KC):
                pr = mc % 2
                proj_mms(s_wo, B1v, bB1c, mc, pair=pr)
                P.op("vector", [bA2], [bU[pr], bXc[mc]],
                     lambda m=mc, p_=pr: V.tensor_tensor(
                         A2v[:, m, :], U[p_][:, 0:HALF], Xv[:, m, :],
                         ALU.add))
                ln_copy(A2v, bA2, B2, bB2c, mc, act=True)
                ln_sq(B2, bB2c, BR, bBRc, mc, dve=True)
            pre["u1"] = stream_weight(wuT[0, :, 1024:2048])
            ln_reduce(B2, bB2c, BR, bBRc)

            P.op("vector", [bD1], [bU[2]],
                 lambda: V.tensor_copy(D1[:, :], U[2][:, 0:HALF]))
            P.op("vector", [bDK[0]], [bU[3]],
                 lambda: V.tensor_copy(DK0[:, :], U[3][:, 0:HALF]))
            for mc in range(KC):
                P.op("vector", [bXc[mc]], [bA2, bD1],
                     lambda m=mc: V.tensor_tensor(
                         Xv[:, m, 0:HALF], A2v[:, m, :], D1[:, :],
                         ALU.subtract))
                P.op("vector", [bXc[mc]], [bXc[mc], bDK[0]],
                     lambda m=mc: V.tensor_tensor(
                         Xv[:, m, 0:HALF], Xv[:, m, 0:HALF], DK0[:, :],
                         ALU.mult))
                P.op("scalar", [bY1c[mc]], [bXc[mc]],
                     lambda m=mc: S.activation(Y1v[:, m, :], Xv[:, m, 0:HALF],
                                               AF.Copy))
            return pre

        def moe_phase(half_idx, next_pro=None, next_r=None, pre=None):
            final = next_pro is None
            accv = Xv
            scores = A2[0:8, 0:HALF]
            biased = A2[0:8, HALF:2 * HALF]
            wselEM = A2[0:8, 2 * HALF:3 * HALF]
            stt = A2[:, 4 * HALF:4 * HALF + 64]
            wsel = A2[:, 4 * HALF + 128:4 * HALF + 192]

            for nt in range(NT):
                for kc in range(KC):
                    P.op("tensor", [bU[2]], [bGT, bXc[kc]],
                         lambda n=nt, k=kc: T.matmul(
                             U[2][0:8, n * 512:(n + 1) * 512],
                             GT[:, k * 8:(k + 1) * 8],
                             accv[:, k, n * 512:(n + 1) * 512],
                             start=(k == 0), stop=(k == KC - 1)))
            P.op("scalar", [bA2], [bU[2]],
                 lambda: S.activation(scores, U[2][0:8, 0:HALF], AF.Sigmoid))
            P.op("vector", [bA2], [bA2, bEB],
                 lambda: V.tensor_single_scalar(biased, scores, EB[:, 0:1],
                                                ALU.add))
            nsl = HALF // 128
            for s_ in range(nsl):
                P.op("tensor", [bU[3]], [bA2, bIDT],
                     lambda s2=s_: T.transpose(
                         U[3][:, s2 * 8:s2 * 8 + 8],
                         biased[:, s2 * 128:(s2 + 1) * 128], IDT[0:8, 0:8]))
            P.op("vector", [bA2], [bU[3]],
                 lambda: V.tensor_copy(stt, U[3][:, 0:8 * nsl]))
            for s_ in range(nsl):
                stB = stt[:, s_ * 8:s_ * 8 + 8]
                P.op("vector", [bMR1], [bA2, bMR1],
                     lambda sb_=stB: V.reduce_max(MR1[:, 0:1], sb_, AX.X))
                P.op("vector", [bTB2], [bMR1, bA2],
                     lambda sb_=stB: V.match_replace(TB2[:, :], MR1[:, 0:8], sb_,
                                                     NEG))
                P.op("vector", [bMR1], [bTB2, bMR1],
                     lambda: V.reduce_max(MR1[:, 0:1], TB2[:, :], AX.X))
                P.op("vector", [bTB3], [bMR1, bTB2],
                     lambda: V.match_replace(TB3[:, :], MR1[:, 0:8], TB2[:, :],
                                             NEG))
                P.op("vector", [bA2], [bA2, bTB3],
                     lambda s2=s_, sb_=stB: V.tensor_tensor(
                         wsel[:, s2 * 8:(s2 + 1) * 8], sb_, TB3[:, :],
                         ALU.not_equal))
            for s_ in range(nsl):
                P.op("tensor", [bU[2]], [bA2, bIDT],
                     lambda s2=s_: T.transpose(
                         U[2][0:8, s2 * 128:(s2 + 1) * 128],
                         wsel[:, s2 * 8:(s2 + 1) * 8], IDT[:, :]))
            P.op("vector", [bA2], [bU[2]],
                 lambda: V.tensor_copy(wselEM, U[2][0:8, 0:HALF]))
            P.op("vector", [bKV[0]], [bA2],
                 lambda: V.tensor_mul(KV0[0:8, 0:HALF], wselEM, scores))

            def emit_wb_ye(e):
                for nt in range(NT):
                    P.op("tensor", [bU[2]], [bOHB, bKV[0]],
                         lambda n=nt, e2=e: T.matmul(
                             U[2][:, n * 512:(n + 1) * 512],
                             OHB[:, e2 * 128:(e2 + 1) * 128],
                             KV0[0:8, n * 512:(n + 1) * 512],
                             start=True, stop=True))
                P.op("vector", [bWB], [bU[2]],
                     lambda: V.tensor_copy(WBt[:, :], U[2][:, 0:HALF]))
                for ci in range(KC):
                    if e == 0:
                        P.op("vector", [bB1c[ci]], [bY1c[ci], bWB],
                             lambda c=ci: V.tensor_tensor(
                                 B1v[:, c, :], Y1v[:, c, :], WBt[:, :],
                                 ALU.mult))
                    else:
                        P.op("gpsimd", [bB1c[ci]], [bY1c[ci], bWB],
                             lambda c=ci: G.tensor_tensor(
                                 B1v[:, c, :], Y1v[:, c, :], WBt[:, :],
                                 ALU.mult))

            nph = NE * 2
            wu_src = [wuT[e, :, hh * 1024:(hh + 1) * 1024]
                      for e in range(NE) for hh in range(2)]
            wd_src = [wdT[e, hh * 1024:(hh + 1) * 1024, :]
                      for e in range(NE) for hh in range(2)]
            slot_u = [None] * nph
            slot_d = [None] * nph
            if pre is not None:
                slot_u[0], slot_d[0], slot_u[1] = (pre["u0"], pre["d0"],
                                                   pre["u1"])
            else:
                slot_u[0] = stream_weight(wu_src[0])
                slot_d[0] = stream_weight(wd_src[0])
                slot_u[1] = stream_weight(wu_src[1])
            emit_wb_ye(0)
            nxt = {}
            for p in range(nph):
                e, hh = divmod(p, 2)
                bhc = bB2c if hh == 0 else bBRc
                hv = B2v if hh == 0 else BRv
                wu_i, wd_i = slot_u[p], slot_d[p]
                for mc in range(KC):
                    pr = mc % 2
                    for nt in range(NT):
                        for kc in range(KC):
                            mm(pr, nt, Wv_[wu_i][:, kc, mc * 128:(mc + 1) * 128],
                               B1v[:, kc, nt * 512:(nt + 1) * 512], bW[wu_i],
                               bB1c[kc], start=(kc == 0), stop=(kc == KC - 1))
                    tr, btr = TRS[mc % 2], bTR[mc % 2]
                    P.op("scalar", [btr], [bU[pr]],
                         lambda p_=pr, t2=tr: S.activation(t2[:, :],
                                                           U[p_][:, 0:HALF],
                                                           AF.Relu))
                    P.op("vector", [bhc[mc]], [btr],
                         lambda m=mc, t2=tr, h2=hv: V.tensor_tensor(
                             h2[:, m, :], t2[:, :], t2[:, :], ALU.mult))
                if hh == 1 and e + 1 < NE:
                    emit_wb_ye(e + 1)
                if p + 1 < nph:
                    slot_d[p + 1] = stream_weight(wd_src[p + 1])
                elif not final:
                    nxt["wk"] = stream_weight(wkT)
                for cc in range(KC):
                    pr = cc % 2
                    for nt in range(NT):
                        for kc in range(KC):
                            mm(pr, nt, Wv_[wd_i][:, kc, cc * 128:(cc + 1) * 128],
                               hv[:, kc, nt * 512:(nt + 1) * 512], bW[wd_i],
                               bhc[kc], start=(kc == 0), stop=(kc == KC - 1))
                    P.op("vector", [bXc[cc]], [bU[pr], bXc[cc]],
                         lambda c2=cc, p_=pr: V.tensor_tensor(
                             Xv[:, c2, 0:HALF], U[p_][:, 0:HALF],
                             Xv[:, c2, 0:HALF], ALU.add))
                    if p == nph - 1:
                        P.op("scalar", [bY1c[cc]], [bXc[cc]],
                             lambda m=cc: S.activation(Y1v[:, m, :],
                                                       Xv[:, m, 0:HALF],
                                                       AF.Copy))
                        if final:
                            ln_sq(Y1B, bY1c, B2, bB2c, cc, dve=False)
                if p + 2 < nph:
                    slot_u[p + 2] = stream_weight(wu_src[p + 2])
                elif p == nph - 2 and not final:
                    nxt["wv"] = stream_weight(wvT)
                elif p == nph - 1 and not final:
                    nxt["wr"] = stream_weight(wrT)
            if not final:
                for mc in range(KC):
                    ln_sq(Y1B, bY1c, BR, bBRc, mc, dve=(mc < 4))

            if next_pro is not None:
                next_pro()


            if final:
                ln_reduce(Y1B, bY1c, B2, bB2c)
            else:
                ln_reduce(Y1B, bY1c, BR, bBRc)
            if next_r is not None:
                next_r()
            def tail():
                gs = []

                def odma(c0, c1):
                    g = Buf("A2g")
                    g.w = bA2.w
                    go = Buf("OUTg")
                    P.op("sync", [go], [g],
                         lambda h=half_idx, a=c0, b=c1: SY.dma_start(
                             out_r[:, a:b, h * HALF:(h + 1) * HALF],
                             A2v[:, a:b, :]), dma=True)
                    gs.append(g)

                for mc in range(KC):
                    P.op("vector", [bA2], [bY1c[mc], bU[2]],
                         lambda m=mc: V.tensor_tensor(A2v[:, m, :], Y1v[:, m, :],
                                                      U[2][:, 0:HALF],
                                                      ALU.subtract))
                    P.op("vector", [bA2], [bA2, bU[3]],
                         lambda m=mc: V.tensor_tensor(A2v[:, m, :], A2v[:, m, :],
                                                      U[3][:, 0:HALF],
                                                      ALU.mult))
                    if mc == 3:
                        odma(0, 4)
                    elif mc == 5:
                        odma(4, 6)
                odma(6, 8)
                for g in gs:
                    for s, v in g.r.items():
                        if bA2.r.get(s, 0) < v:
                            bA2.r[s] = v
            return nxt, tail

        s_wk = stream_weight(wkT)
        s_wv = stream_weight(wvT)
        attn_phase(0, "warm", True, {"wk": s_wk, "wv": s_wv})
        s_wr = stream_weight(wrT)
        pre1 = attn_phase(WARM, "full", False,
                          {"wk": s_wk, "wv": s_wv, "wr": s_wr})
        nxt, tail1 = moe_phase(
            0,
            next_pro=lambda: attn_prologue(WARM + HALF, False, with_r=False),
            next_r=lambda: mixes([(BRv, bBRc, "tmr1")]),
            pre=pre1)
        tail1()
        pre2 = attn_phase(WARM + HALF, "full", False, nxt, prologue_done=True)
        _, tail2 = moe_phase(1, pre=pre2)
        tail2()

        P.barrier_wait("sync", [(s, P.count[s])
                                for s in ("dSP0", "dSP1", "dSP2")])
        P.barrier_wait("gpsimd", [(s, P.count[s])
                                  for s in ("dGP0", "dGP1", "dGP2",
                                            "dGP3", "dGP4", "dGP5")])
        P.barrier_wait("scalar", [("dACT0", P.count["dACT0"])])

        P.check_deadlock()

        blk = st.enter_context(nc.Block())

        @blk.tensor
        def _(eng):
            P.emit(eng, "tensor")

        @blk.vector
        def _(eng):
            P.emit(eng, "vector")

        @blk.scalar
        def _(eng):
            P.emit(eng, "scalar")

        @blk.gpsimd
        def _(eng):
            P.emit(eng, "gpsimd")

        @blk.sync
        def _(eng):
            P.emit(eng, "sync")

    return nc


_NC_CACHE = {}


def _get_nc():
    if "nc" not in _NC_CACHE:
        _NC_CACHE["nc"] = build_nc()
    return _NC_CACHE["nc"]


def kernel(L0_attn_x, L0_attn_time_mix_k, L0_attn_time_mix_v, L0_attn_time_mix_r,
           L0_attn_time_decay, L0_attn_time_first, L0_attn_Wk, L0_attn_Wv,
           L0_attn_Wr, L0_attn_Wo, L0_attn_ln_g, L0_attn_ln_b, L0_ffn_gate_w,
           L0_ffn_w_up, L0_ffn_w_down, L0_ffn_expert_bias):
    x = np.asarray(L0_attn_x, dtype=np.float32).reshape(NCORES * TC, C)

    def vec_layout(v):
        return np.ascontiguousarray(np.asarray(v, np.float32).reshape(KC, 128).T)

    vecs = np.concatenate([
        vec_layout(L0_attn_time_mix_k), vec_layout(L0_attn_time_mix_v),
        vec_layout(L0_attn_time_mix_r), vec_layout(L0_attn_time_decay),
        vec_layout(L0_attn_time_first), vec_layout(L0_attn_ln_g),
        vec_layout(L0_attn_ln_b)], axis=1).astype(np.float32)

    gate = np.asarray(L0_ffn_gate_w, np.float32)
    gateT = np.ascontiguousarray(
        gate.T.reshape(KC, 128, NE).transpose(1, 0, 2).reshape(128, KC * NE))

    cst = np.zeros((128, 408), np.float32)
    cst[:, 0:56] = vecs
    cst[:, 80:144] = gateT
    cst[:, 144:272] = np.eye(128, dtype=np.float32)
    cst[0:8, 272] = np.asarray(L0_ffn_expert_bias, np.float32).reshape(NE)
    cst[0, 273:401] = 1.0
    cstb = np.zeros((128, 1028), np.float32)
    cstb[:, 0] = 1.0
    for e in range(NE):
        cstb[e, 1 + e * 128:1 + (e + 1) * 128] = 1.0

    shared = {
        "wkT": np.ascontiguousarray(np.asarray(L0_attn_Wk, np.float32).T),
        "wvT": np.ascontiguousarray(np.asarray(L0_attn_Wv, np.float32).T),
        "wrT": np.ascontiguousarray(np.asarray(L0_attn_Wr, np.float32).T),
        "woT": np.ascontiguousarray(np.asarray(L0_attn_Wo, np.float32).T),
        "wuT": np.ascontiguousarray(
            np.asarray(L0_ffn_w_up, np.float32).transpose(0, 2, 1)),
        "wdT": np.ascontiguousarray(
            np.asarray(L0_ffn_w_down, np.float32).transpose(0, 2, 1)),
        "cst": cst,
        "cstb": cstb,
    }

    in_maps = []
    for c in range(NCORES):
        lo = c * TC - (WARM + 1)
        hi = (c + 1) * TC
        xs = np.zeros((WARM + 1 + TC, C), np.float32)
        a = max(lo, 0)
        xs[a - lo:hi - lo] = x[a:hi]
        m = dict(shared)
        m["xT"] = np.ascontiguousarray(xs.T)
        in_maps.append(m)

    nc = _get_nc()
    res = run_bass_kernel_spmd(nc, in_maps, core_ids=list(range(NCORES)))
    out = np.concatenate([np.asarray(r["out"]).T for r in res.results], axis=0)
    return out.reshape(1, NCORES * TC, C).astype(np.float32)


# version 42
# speedup vs baseline: 1.0044x; 1.0044x over previous
import contextlib
import numpy as np

import concourse.bass as bass
import concourse.mybir as mybir
from concourse.bass_utils import run_bass_kernel_spmd

F32 = mybir.dt.float32
BF16 = mybir.dt.bfloat16
AF = mybir.ActivationFunctionType
ALU = mybir.AluOpType
AX = mybir.AxisListType

NCORES = 8
C = 1024
KC = 8
TC = 2048
HALF = 1024
WARM = 512
NE = 8
HID = 2048
LN_EPS = 1e-5
NEG = -1e30
NT = HALF // 512

ENG = ("tensor", "vector", "scalar", "gpsimd", "sync")


class Buf:
    __slots__ = ("name", "w", "r")

    def __init__(self, name):
        self.name = name
        self.w = None
        self.r = {}


class Prog:

    def __init__(self, nc, sems):
        self.nc = nc
        self.sems = sems
        self.count = {k: 0 for k in sems}
        self.sat = {e: {} for e in ENG}
        self.ops = {e: [] for e in ENG}
        self.engsem = {"tensor": "ePE", "vector": "eDVE", "scalar": "eACT",
                       "gpsimd": "ePOOL", "sync": "eSP"}
        self.dmalanes = {"gpsimd": ["dGP0", "dGP1", "dGP2", "dGP3",
                                    "dGP4", "dGP5"],
                         "sync": ["dSP0", "dSP1", "dSP2"],
                         "scalar": ["dACT0"]}
        self.dmaidx = {k: 0 for k in self.dmalanes}

    def op(self, eng, outs, ins, fn, dma=False):
        waits = {}

        def need(m):
            if m is not None:
                s, v = m
                if v > waits.get(s, 0):
                    waits[s] = v

        for b in ins:
            need(b.w)
        for b in outs:
            need(b.w)
            for s, v in b.r.items():
                need((s, v))
        if dma:
            lanes = self.dmalanes[eng]
            cs = lanes[self.dmaidx[eng] % len(lanes)]
            self.dmaidx[eng] += 1
            inc = 16
            if self.count[cs] > 0:
                need((cs, self.count[cs]))
        else:
            cs = self.engsem[eng]
            inc = 1
        wl = []
        for s, v in sorted(waits.items()):
            if self.sat[eng].get(s, 0) >= v:
                continue
            if eng == "tensor" and s == "ePE":
                continue
            wl.append((s, v))
            self.sat[eng][s] = v
        self.count[cs] += inc
        v = self.count[cs]
        self.ops[eng].append((wl, fn, (cs, inc)))
        for b in outs:
            b.w = (cs, v)
            b.r = {}
        for b in ins:
            if b.r.get(cs, 0) < v:
                b.r[cs] = v
        return (cs, v)

    def barrier_wait(self, eng, milestones):
        wl = []
        for s, v in milestones:
            if self.sat[eng].get(s, 0) >= v:
                continue
            wl.append((s, v))
            self.sat[eng][s] = v
        if wl:
            self.ops[eng].append((wl, None, None))

    def check_deadlock(self):
        heads = {e: 0 for e in ENG}
        vals = {k: 0 for k in self.sems}
        total = sum(len(v) for v in self.ops.values())
        done = 0
        progress = True
        while progress:
            progress = False
            for e in ENG:
                while heads[e] < len(self.ops[e]):
                    wl, fn, incs = self.ops[e][heads[e]]
                    if any(vals[s] < v for s, v in wl):
                        break
                    if incs is not None:
                        vals[incs[0]] += incs[1]
                    heads[e] += 1
                    done += 1
                    progress = True
        if done != total:
            detail = []
            for e in ENG:
                if heads[e] < len(self.ops[e]):
                    wl = self.ops[e][heads[e]][0]
                    blocked = [(s, v, vals[s]) for s, v in wl if vals[s] < v]
                    detail.append(f"{e}@{heads[e]}/{len(self.ops[e])} waits {blocked}")
            raise RuntimeError("deadlock in emitted program: " + "; ".join(detail))

    def emit(self, eng_obj, eng):
        for wl, fn, incs in self.ops[eng]:
            for s, v in wl:
                eng_obj.wait_ge(self.sems[s], v)
            if fn is not None:
                inst = fn()
                if incs is not None:
                    inst.then_inc(self.sems[incs[0]], incs[1])


def col_bcast(t, ci, n, free):
    return bass.AP(t, ci, [[free, 128], [0, n]])


def build_nc():
    nc = bass.Bass("TRN2", target_bir_lowering=False)

    xT = nc.dram_tensor("xT", [C, WARM + 1 + TC], F32, kind="ExternalInput").ap()
    wkT = nc.dram_tensor("wkT", [C, C], F32, kind="ExternalInput").ap()
    wvT = nc.dram_tensor("wvT", [C, C], F32, kind="ExternalInput").ap()
    wrT = nc.dram_tensor("wrT", [C, C], F32, kind="ExternalInput").ap()
    woT = nc.dram_tensor("woT", [C, C], F32, kind="ExternalInput").ap()
    wuT = nc.dram_tensor("wuT", [NE, C, HID], F32, kind="ExternalInput").ap()
    wdT = nc.dram_tensor("wdT", [NE, HID, C], F32, kind="ExternalInput").ap()
    cst_d = nc.dram_tensor("cst", [128, 408], F32, kind="ExternalInput").ap()
    cstb_d = nc.dram_tensor("cstb", [128, 1028], F32, kind="ExternalInput").ap()
    out_d = nc.dram_tensor("out", [C, TC], F32, kind="ExternalOutput").ap()

    xT_r = xT.rearrange("(kc p) t -> p kc t", p=128)
    out_r = out_d.rearrange("(kc p) t -> p kc t", p=128)

    st = contextlib.ExitStack()
    with st:
        def sb(name, shape, dt):
            return st.enter_context(nc.sbuf_tensor(name, shape, dt))

        X = sb("X", [128, KC * HALF], F32)
        A2 = sb("A2", [128, KC * HALF], F32)
        D1 = sb("D1", [128, HALF], F32)
        DK0 = sb("DK0", [128, HALF], F32)
        DK1 = sb("DK1", [128, HALF], F32)
        B1 = sb("B1", [128, KC * HALF], BF16)
        B2 = sb("B2", [128, KC * HALF], BF16)
        BR = sb("BR", [128, KC * HALF], BF16)
        KV0 = sb("KV0", [128, HALF + 8], BF16)
        KV1 = sb("KV1", [128, HALF + 8], BF16)
        Y1B = sb("Y1B", [128, KC * HALF], BF16)
        W0 = sb("W0", [128, KC * 1024], BF16)
        W1 = sb("W1", [128, KC * 1024], BF16)
        W2 = sb("W2", [128, KC * 1024], BF16)
        WBt = sb("WBt", [128, HALF], BF16)
        TR0 = sb("TR0", [128, 1024], BF16)
        TR1 = sb("TR1", [128, 1024], BF16)
        CONST = sb("CONST", [128, 408], F32)
        CONSTB = sb("CONSTB", [128, 1028], BF16)
        VEC = CONST[:, 0:80]
        DEC = sb("DEC", [128, KC], F32)
        GT = CONST[:, 80:144]
        EB = CONST[0:8, 272:273]
        IDT = CONST[:, 144:272]
        OHB = CONSTB[0:8, 1:1025]
        ONR = CONST[0:1, 273:401]
        ONCB = CONSTB[:, 0:1]
        STATE = sb("STATE", [128, KC], F32)
        SSUM, SSSQ, STMP = D1, DK0, DK1
        EPSC = sb("EPSC", [1, 1], F32)
        NHC = sb("NHC", [1, 1], F32)
        MR1 = sb("MR1", [128, 8], F32)
        TB2 = sb("TB2", [128, 8], F32)
        TB3 = sb("TB3", [128, 8], F32)

        U = [st.enter_context(nc.psum_tensor(f"U{i}", [128, 1024], F32))
             for i in range(4)]

        sems = {}
        for s in ("ePE", "eDVE", "eACT", "ePOOL", "eSP",
                  "dGP0", "dGP1", "dGP2", "dGP3", "dGP4", "dGP5",
                  "dSP0", "dSP1", "dSP2", "dACT0"):
            sems[s] = st.enter_context(nc.semaphore(s))

        P = Prog(nc, sems)

        bA2, bD1 = Buf("A2"), Buf("D1")
        bXc = [Buf(f"X.{i}") for i in range(KC)]
        bDK = [Buf("DK0"), Buf("DK1")]
        bB1c = [Buf(f"B1.{i}") for i in range(KC)]
        bB2c = [Buf(f"B2.{i}") for i in range(KC)]
        bBRc = [Buf(f"BR.{i}") for i in range(KC)]
        bKV = [Buf("KV0"), Buf("KV1")]
        bY1c = [Buf(f"Y1B.{i}") for i in range(KC)]
        bWB = Buf("WB")
        bW = [Buf("W0"), Buf("W1"), Buf("W2")]
        bTR = [Buf("TR0"), Buf("TR1")]
        bCONST, bDEC, bCONSTB = Buf("CONST"), Buf("DEC"), Buf("CONSTB")
        bVEC = bGT = bEB = bIDT = bONR = bCONST
        bOHB = bONCB = bCONSTB
        bSTATE = Buf("STATE")
        bSSUM, bSSSQ, bSTMP = bD1, bDK[0], bDK[1]
        bMR1, bTB2, bTB3 = Buf("MR1"), Buf("TB2"), Buf("TB3")
        bU = [Buf(f"U{i}") for i in range(4)]
        bOUT = Buf("OUT")
        WSLOT = [W0, W1, W2]
        TRS = [TR0, TR1]
        DKS = [DK0, DK1]
        KVS = [KV0, KV1]

        V, S, T, G, SY = nc.vector, nc.scalar, nc.tensor, nc.gpsimd, nc.sync

        Xv = X[:, :].rearrange("p (c t) -> p c t", c=KC)
        A2v = A2[:, :].rearrange("p (c t) -> p c t", c=KC)
        B1v = B1[:, :].rearrange("p (c t) -> p c t", c=KC)
        B2v = B2[:, :].rearrange("p (c t) -> p c t", c=KC)
        BRv = BR[:, :].rearrange("p (c t) -> p c t", c=KC)
        Y1v = Y1B[:, :].rearrange("p (c t) -> p c t", c=KC)
        Wv_ = [w[:, :].rearrange("p (c m) -> p c m", c=KC) for w in WSLOT]

        vslice = {name: VEC[:, i * KC:(i + 1) * KC]
                  for i, name in enumerate(
                      ["tmk", "tmv", "tmr", "td", "tf", "lng", "lnb",
                       "tmk1", "tmv1", "tmr1"])}

        P.op("sync", [bCONST], [],
             lambda: SY.dma_start(CONST[:, :], cst_d), dma=True)
        P.op("gpsimd", [bCONSTB], [],
             lambda: G.dma_start(CONSTB[:, :], cstb_d), dma=True)
        P.op("scalar", [bDEC], [bVEC],
             lambda: S.activation(DEC[:, :], vslice["td"], AF.Exp))
        P.op("vector", [bMR1], [], lambda: V.memset(MR1[:, :], NEG))

        bEPSC = Buf("EPSC")
        P.op("vector", [bEPSC], [], lambda: V.memset(EPSC[:, :], LN_EPS))
        bNHC = Buf("NHC")
        P.op("vector", [bNHC], [], lambda: V.memset(NHC[:, :], -0.5))
        for a, b_ in (("tmk1", "tmk"), ("tmv1", "tmv"), ("tmr1", "tmr")):
            P.op("vector", [bVEC], [bVEC],
                 lambda a2=a, b2=b_: V.tensor_single_scalar(
                     vslice[a2], vslice[b2], 1.0, ALU.subtract))

        ring = {"i": 0}

        def stream_weight(src_ap):
            i = ring["i"] % 3
            ring["i"] += 1
            slot, bslot = WSLOT[i], bW[i]
            src = src_ap.rearrange("(kc p) m -> p kc m", p=128)
            P.op("gpsimd", [bslot], [],
                 lambda sl=slot, sr=src: G.dma_start(
                     sl[:, :].rearrange("p (c m) -> p c m", c=KC), sr), dma=True)
            return i

        def mm(ui, nt, lhsT, rhs, lhs_buf, rhs_buf, start, stop):
            M = lhsT.shape[-1]
            P.op("tensor", [bU[ui]], [lhs_buf, rhs_buf],
                 lambda u=ui, n=nt, l=lhsT, r=rhs, m_=M, st_=start, sp=stop:
                 T.matmul(U[u][0:m_, n * 512:n * 512 + r.shape[-1]], l, r,
                          start=st_, stop=sp))

        def proj_mms(ws, rhs_view, rhs_bufs, mc, pair, w=HALF):
            for nt in range(w // 512):
                for kc in range(KC):
                    mm(pair, nt, Wv_[ws][:, kc, mc * 128:(mc + 1) * 128],
                       rhs_view[:, kc, nt * 512:(nt + 1) * 512],
                       bW[ws], rhs_bufs[kc], start=(kc == 0), stop=(kc == KC - 1))

        def ln_copy(src_view, src_buf, bft, bfbs, mc, act=False):
            bfv = bft[:, :].rearrange("p (c t) -> p c t", c=KC)
            if act:
                P.op("scalar", [bfbs[mc]], [src_buf],
                     lambda m=mc, sv=src_view, bv=bfv: S.activation(
                         bv[:, m, :], sv[:, m, 0:HALF], AF.Copy))
            else:
                P.op("vector", [bfbs[mc]], [src_buf],
                     lambda m=mc, sv=src_view, bv=bfv: V.tensor_copy(
                         bv[:, m, :], sv[:, m, 0:HALF]))

        def ln_sq(bft, bfbs, sqt, sqbs, mc, dve=False):
            bfv = bft[:, :].rearrange("p (c t) -> p c t", c=KC)
            sqv = sqt[:, :].rearrange("p (c t) -> p c t", c=KC)
            if dve:
                P.op("vector", [sqbs[mc]], [bfbs[mc]],
                     lambda m=mc, bv=bfv, sv2=sqv: V.tensor_tensor(
                         sv2[:, m, :], bv[:, m, :], bv[:, m, :], ALU.mult))
            else:
                P.op("scalar", [sqbs[mc]], [bfbs[mc]],
                     lambda m=mc, bv=bfv, sv2=sqv: S.activation(
                         sv2[:, m, :], bv[:, m, :], AF.Square))

        def ln_copy_sq(src_view, src_buf, bft, bfbs, sqt, sqbs, mc):
            ln_copy(src_view, src_buf, bft, bfbs, mc)
            ln_sq(bft, bfbs, sqt, sqbs, mc)

        def ln_reduce(bft, bfbs, sqt, sqbs):
            bfv = bft[:, :].rearrange("p (c t) -> p c t", c=KC)
            sqv = sqt[:, :].rearrange("p (c t) -> p c t", c=KC)
            for nt in range(NT):
                for kc in range(KC):
                    mm(2, nt, ONCB[:, 0:1], bfv[:, kc, nt * 512:(nt + 1) * 512],
                       bONCB, bfbs[kc], start=(kc == 0), stop=(kc == KC - 1))
                for kc in range(KC):
                    mm(3, nt, ONCB[:, 0:1], sqv[:, kc, nt * 512:(nt + 1) * 512],
                       bONCB, sqbs[kc], start=(kc == 0), stop=(kc == KC - 1))
            P.op("vector", [bSSUM], [bU[2]],
                 lambda: V.tensor_copy(SSUM[0:1, :], U[2][0:1, 0:HALF]))
            P.op("vector", [bSSSQ], [bU[3]],
                 lambda: V.tensor_copy(SSSQ[0:1, :], U[3][0:1, 0:HALF]))
            inv = 1.0 / C
            P.op("vector", [bSSUM], [bSSUM],
                 lambda: V.tensor_single_scalar(SSUM[0:1, :], SSUM[0:1, :], inv,
                                                ALU.mult))
            P.op("vector", [bSTMP], [bSSUM],
                 lambda: V.tensor_mul(STMP[0:1, :], SSUM[0:1, :], SSUM[0:1, :]))
            P.op("vector", [bSSSQ], [bSSSQ, bSTMP],
                 lambda: V.scalar_tensor_tensor(SSSQ[0:1, :], SSSQ[0:1, :], inv,
                                                STMP[0:1, :], ALU.mult,
                                                ALU.subtract))
            P.op("scalar", [bSTMP], [bSSSQ, bEPSC],
                 lambda: S.activation(STMP[0:1, :], SSSQ[0:1, :], AF.Ln,
                                      bias=EPSC[0:1, 0:1]))
            P.op("scalar", [bSTMP], [bSTMP, bNHC],
                 lambda: S.activation(STMP[0:1, :], STMP[0:1, :], AF.Exp,
                                      scale=NHC[0:1, 0:1]))
            for nt in range(NT):
                mm(2, nt, ONR[0:1, :], SSUM[0:1, nt * 512:(nt + 1) * 512],
                   bONR, bSSUM, start=True, stop=True)
                mm(3, nt, ONR[0:1, :], STMP[0:1, nt * 512:(nt + 1) * 512],
                   bONR, bSTMP, start=True, stop=True)

        mix_cfg = {"c0": 0}

        def mixes(targets, w=HALF):
            c0 = mix_cfg["c0"]
            for ci in range(KC):
                pool = ci % 2 == 1
                d, bd = (DK0, bDK[0]) if pool else (D1, bD1)
                xs, bxs = KVS[ci % 2], bKV[ci % 2]
                P.op("gpsimd", [bxs], [],
                     lambda c=ci, x2=xs, w2=w, cc=c0: G.dma_start(
                         x2[:, 0:w2 + 1], xT_r[:, c, cc:cc + 1 + w2]), dma=True)
                P.op("vector", [bd], [bxs],
                     lambda d2=d, x2=xs, w2=w: V.tensor_sub(
                         d2[:, 0:w2], x2[:, 1:w2 + 1], x2[:, 0:w2]))
                for dv, dbufs, tm in targets:
                    P.op("vector", [dbufs[ci]], [bd, bxs, bVEC],
                         lambda c=ci, d2=d, x2=xs, dv2=dv, t_=tm, w2=w:
                         V.scalar_tensor_tensor(
                             dv2[:, c, 0:w2], d2[:, 0:w2], vslice[t_][:, c:c + 1],
                             x2[:, 1:w2 + 1], ALU.mult, ALU.add))

        def attn_prologue(x_col0, first, w=HALF, with_r=True):
            mix_cfg["c0"] = x_col0
            tg = [(B1v, bB1c, "tmk1"), (B2v, bB2c, "tmv1")]
            if with_r:
                tg.append((BRv, bBRc, "tmr1"))
            mixes(tg, w)

        def attn_phase(x_col0, kind, first, slots, prologue_done=False):
            warm = (kind == "warm")
            if not prologue_done:
                attn_prologue(x_col0, first, w=WARM if warm else HALF,
                              with_r=not warm)
            if not warm:
                gx = Buf("XLg")
                gx.w = bU[0].w
                for g2 in range(2):
                    P.op("sync", [bXc[g2 * 4 + i] for i in range(4)], [gx],
                         lambda c0=x_col0, g3=g2: SY.dma_start(
                             Xv[:, g3 * 4:(g3 + 1) * 4, :],
                             xT_r[:, g3 * 4:(g3 + 1) * 4,
                                  c0 + 1:c0 + 1 + HALF]), dma=True)

            w = WARM if warm else HALF
            for mc in range(KC):
                dk, bdk = DKS[mc % 2], bDK[mc % 2]
                kvt, bkv = KVS[mc % 2], bKV[mc % 2]
                uk = (mc % 2) * 2 if warm else 0
                uv = uk + 1
                proj_mms(slots["wk"], B1v, bB1c, mc, pair=uk, w=w)
                proj_mms(slots["wv"], B2v, bB2c, mc, pair=uv, w=w)
                tr, btr = TRS[mc % 2], bTR[mc % 2]
                P.op("scalar", [bdk], [bU[uk]],
                     lambda d=dk, w2=w, u=uk: S.activation(
                         d[:, 0:w2], U[u][:, 0:w2], AF.Copy))
                P.op("scalar", [btr], [bU[uv]],
                     lambda t2=tr, w2=w, u=uv: S.activation(
                         t2[:, 0:w2], U[u][:, 0:w2], AF.Copy))
                P.op("gpsimd", [bkv], [btr, bdk],
                     lambda d=dk, t2=tr, kv=kvt, w2=w: G.tensor_tensor(
                         kv[:, 0:w2], t2[:, 0:w2], d[:, 0:w2], ALU.mult))
                init = 0.0 if (warm and first) else STATE[:, mc:mc + 1]
                ins_l = [bkv, bDEC] + ([] if (warm and first) else [bSTATE])
                P.op("vector", [bA2], ins_l,
                     lambda m=mc, kv=kvt, i=init, w2=w: V.tensor_tensor_scan(
                         A2v[:, m, 0:w2], col_bcast(DEC, m, w2, KC), kv[:, 0:w2],
                         i, ALU.mult, ALU.add))
                P.op("vector", [bSTATE], [bA2],
                     lambda m=mc, w2=w: V.tensor_copy(
                         STATE[:, m:m + 1],
                         A2[:, m * HALF + w2 - 1:m * HALF + w2]))
                if not warm:
                    P.op("vector", [bA2], [bA2, bkv, bVEC],
                         lambda m=mc, kv=kvt: V.scalar_tensor_tensor(
                             A2v[:, m, :], kv[:, 0:HALF], vslice["tf"][:, m:m + 1],
                             A2v[:, m, :], ALU.mult, ALU.add))
            if warm:
                return
            s_wo = stream_weight(woT)
            pre = {"u0": stream_weight(wuT[0, :, 0:1024])}

            for mc in range(KC):
                pr = mc % 2
                proj_mms(slots["wr"], BRv, bBRc, mc, pair=pr)
                P.op("vector", [bB1c[mc]], [bU[pr]],
                     lambda m=mc, p_=pr: V.tensor_copy(B1v[:, m, :],
                                                       U[p_][:, 0:HALF]))
                P.op("vector", [bA2], [bA2, bB1c[mc]],
                     lambda m=mc: V.tensor_tensor(A2v[:, m, :], A2v[:, m, :],
                                                  B1v[:, m, :], ALU.mult))
                ln_copy(A2v, bA2, B2, bB2c, mc, act=True)
            pre["d0"] = stream_weight(wdT[0, 0:1024, :])
            for mc in range(KC):
                ln_sq(B2, bB2c, BR, bBRc, mc, dve=(mc < 4))
            ln_reduce(B2, bB2c, BR, bBRc)

            P.op("vector", [bTR[0]], [bU[2]],
                 lambda: V.tensor_copy(TR0[:, :], U[2][:, 0:HALF]))
            P.op("vector", [bTR[1]], [bU[3]],
                 lambda: V.tensor_copy(TR1[:, :], U[3][:, 0:HALF]))
            for mc in range(KC):
                P.op("vector", [bB2c[mc]], [bB2c[mc], bTR[0]],
                     lambda m=mc: V.tensor_tensor(B2v[:, m, :], B2v[:, m, :],
                                                  TR0[:, :], ALU.subtract))
                P.op("vector", [bB2c[mc]], [bB2c[mc], bTR[1]],
                     lambda m=mc: V.tensor_tensor(B2v[:, m, :], B2v[:, m, :],
                                                  TR1[:, :], ALU.mult))
                P.op("vector", [bB1c[mc]], [bB2c[mc], bVEC],
                     lambda m=mc: V.tensor_scalar(
                         B1v[:, m, :], B2v[:, m, :], vslice["lng"][:, m:m + 1],
                         vslice["lnb"][:, m:m + 1], ALU.mult, ALU.add))
            for mc in range(KC):
                pr = mc % 2
                proj_mms(s_wo, B1v, bB1c, mc, pair=pr)
                P.op("vector", [bA2], [bU[pr], bXc[mc]],
                     lambda m=mc, p_=pr: V.tensor_tensor(
                         A2v[:, m, :], U[p_][:, 0:HALF], Xv[:, m, :],
                         ALU.add))
                ln_copy(A2v, bA2, B2, bB2c, mc, act=True)
                ln_sq(B2, bB2c, BR, bBRc, mc, dve=True)
            pre["u1"] = stream_weight(wuT[0, :, 1024:2048])
            ln_reduce(B2, bB2c, BR, bBRc)

            P.op("vector", [bD1], [bU[2]],
                 lambda: V.tensor_copy(D1[:, :], U[2][:, 0:HALF]))
            P.op("vector", [bDK[0]], [bU[3]],
                 lambda: V.tensor_copy(DK0[:, :], U[3][:, 0:HALF]))
            for mc in range(KC):
                P.op("vector", [bXc[mc]], [bA2, bD1],
                     lambda m=mc: V.tensor_tensor(
                         Xv[:, m, 0:HALF], A2v[:, m, :], D1[:, :],
                         ALU.subtract))
                P.op("vector", [bXc[mc]], [bXc[mc], bDK[0]],
                     lambda m=mc: V.tensor_tensor(
                         Xv[:, m, 0:HALF], Xv[:, m, 0:HALF], DK0[:, :],
                         ALU.mult))
                P.op("scalar", [bY1c[mc]], [bXc[mc]],
                     lambda m=mc: S.activation(Y1v[:, m, :], Xv[:, m, 0:HALF],
                                               AF.Copy))
            return pre

        def moe_phase(half_idx, next_pro=None, next_r=None, pre=None):
            final = next_pro is None
            accv = Xv
            scores = A2[0:8, 0:HALF]
            biased = A2[0:8, HALF:2 * HALF]
            wselEM = A2[0:8, 2 * HALF:3 * HALF]
            stt = A2[:, 4 * HALF:4 * HALF + 64]
            wsel = A2[:, 4 * HALF + 128:4 * HALF + 192]

            for nt in range(NT):
                for kc in range(KC):
                    P.op("tensor", [bU[2]], [bGT, bXc[kc]],
                         lambda n=nt, k=kc: T.matmul(
                             U[2][0:8, n * 512:(n + 1) * 512],
                             GT[:, k * 8:(k + 1) * 8],
                             accv[:, k, n * 512:(n + 1) * 512],
                             start=(k == 0), stop=(k == KC - 1)))
            P.op("scalar", [bA2], [bU[2]],
                 lambda: S.activation(scores, U[2][0:8, 0:HALF], AF.Sigmoid))
            P.op("vector", [bA2], [bA2, bEB],
                 lambda: V.tensor_single_scalar(biased, scores, EB[:, 0:1],
                                                ALU.add))
            nsl = HALF // 128
            for s_ in range(nsl):
                P.op("tensor", [bU[3]], [bA2, bIDT],
                     lambda s2=s_: T.transpose(
                         U[3][:, s2 * 8:s2 * 8 + 8],
                         biased[:, s2 * 128:(s2 + 1) * 128], IDT[0:8, 0:8]))
            P.op("vector", [bA2], [bU[3]],
                 lambda: V.tensor_copy(stt, U[3][:, 0:8 * nsl]))
            for s_ in range(nsl):
                stB = stt[:, s_ * 8:s_ * 8 + 8]
                P.op("vector", [bMR1], [bA2, bMR1],
                     lambda sb_=stB: V.reduce_max(MR1[:, 0:1], sb_, AX.X))
                P.op("vector", [bTB2], [bMR1, bA2],
                     lambda sb_=stB: V.match_replace(TB2[:, :], MR1[:, 0:8], sb_,
                                                     NEG))
                P.op("vector", [bMR1], [bTB2, bMR1],
                     lambda: V.reduce_max(MR1[:, 0:1], TB2[:, :], AX.X))
                P.op("vector", [bTB3], [bMR1, bTB2],
                     lambda: V.match_replace(TB3[:, :], MR1[:, 0:8], TB2[:, :],
                                             NEG))
                P.op("vector", [bA2], [bA2, bTB3],
                     lambda s2=s_, sb_=stB: V.tensor_tensor(
                         wsel[:, s2 * 8:(s2 + 1) * 8], sb_, TB3[:, :],
                         ALU.not_equal))
            for s_ in range(nsl):
                P.op("tensor", [bU[2]], [bA2, bIDT],
                     lambda s2=s_: T.transpose(
                         U[2][0:8, s2 * 128:(s2 + 1) * 128],
                         wsel[:, s2 * 8:(s2 + 1) * 8], IDT[:, :]))
            P.op("vector", [bA2], [bU[2]],
                 lambda: V.tensor_copy(wselEM, U[2][0:8, 0:HALF]))
            P.op("vector", [bKV[0]], [bA2],
                 lambda: V.tensor_mul(KV0[0:8, 0:HALF], wselEM, scores))

            def emit_wb_ye(e):
                for nt in range(NT):
                    P.op("tensor", [bU[2]], [bOHB, bKV[0]],
                         lambda n=nt, e2=e: T.matmul(
                             U[2][:, n * 512:(n + 1) * 512],
                             OHB[:, e2 * 128:(e2 + 1) * 128],
                             KV0[0:8, n * 512:(n + 1) * 512],
                             start=True, stop=True))
                P.op("vector", [bWB], [bU[2]],
                     lambda: V.tensor_copy(WBt[:, :], U[2][:, 0:HALF]))
                for ci in range(KC):
                    if e == 0:
                        P.op("vector", [bB1c[ci]], [bY1c[ci], bWB],
                             lambda c=ci: V.tensor_tensor(
                                 B1v[:, c, :], Y1v[:, c, :], WBt[:, :],
                                 ALU.mult))
                    else:
                        P.op("gpsimd", [bB1c[ci]], [bY1c[ci], bWB],
                             lambda c=ci: G.tensor_tensor(
                                 B1v[:, c, :], Y1v[:, c, :], WBt[:, :],
                                 ALU.mult))

            nph = NE * 2
            wu_src = [wuT[e, :, hh * 1024:(hh + 1) * 1024]
                      for e in range(NE) for hh in range(2)]
            wd_src = [wdT[e, hh * 1024:(hh + 1) * 1024, :]
                      for e in range(NE) for hh in range(2)]
            slot_u = [None] * nph
            slot_d = [None] * nph
            if pre is not None:
                slot_u[0], slot_d[0], slot_u[1] = (pre["u0"], pre["d0"],
                                                   pre["u1"])
            else:
                slot_u[0] = stream_weight(wu_src[0])
                slot_d[0] = stream_weight(wd_src[0])
                slot_u[1] = stream_weight(wu_src[1])
            emit_wb_ye(0)
            nxt = {}
            for p in range(nph):
                e, hh = divmod(p, 2)
                bhc = bB2c if hh == 0 else bBRc
                hv = B2v if hh == 0 else BRv
                wu_i, wd_i = slot_u[p], slot_d[p]
                for mc in range(KC):
                    pr = mc % 2
                    for nt in range(NT):
                        for kc in range(KC):
                            mm(pr, nt, Wv_[wu_i][:, kc, mc * 128:(mc + 1) * 128],
                               B1v[:, kc, nt * 512:(nt + 1) * 512], bW[wu_i],
                               bB1c[kc], start=(kc == 0), stop=(kc == KC - 1))
                    tr, btr = TRS[mc % 2], bTR[mc % 2]
                    P.op("scalar", [btr], [bU[pr]],
                         lambda p_=pr, t2=tr: S.activation(t2[:, :],
                                                           U[p_][:, 0:HALF],
                                                           AF.Relu))
                    P.op("vector", [bhc[mc]], [btr],
                         lambda m=mc, t2=tr, h2=hv: V.tensor_tensor(
                             h2[:, m, :], t2[:, :], t2[:, :], ALU.mult))
                if hh == 1 and e + 1 < NE:
                    emit_wb_ye(e + 1)
                if p + 1 < nph:
                    slot_d[p + 1] = stream_weight(wd_src[p + 1])
                elif not final:
                    nxt["wk"] = stream_weight(wkT)
                for cc in range(KC):
                    pr = cc % 2
                    for nt in range(NT):
                        for kc in range(KC):
                            mm(pr, nt, Wv_[wd_i][:, kc, cc * 128:(cc + 1) * 128],
                               hv[:, kc, nt * 512:(nt + 1) * 512], bW[wd_i],
                               bhc[kc], start=(kc == 0), stop=(kc == KC - 1))
                    P.op("vector", [bXc[cc]], [bU[pr], bXc[cc]],
                         lambda c2=cc, p_=pr: V.tensor_tensor(
                             Xv[:, c2, 0:HALF], U[p_][:, 0:HALF],
                             Xv[:, c2, 0:HALF], ALU.add))
                    if p == nph - 1:
                        P.op("scalar", [bY1c[cc]], [bXc[cc]],
                             lambda m=cc: S.activation(Y1v[:, m, :],
                                                       Xv[:, m, 0:HALF],
                                                       AF.Copy))
                        if final:
                            ln_sq(Y1B, bY1c, B2, bB2c, cc, dve=False)
                if p + 2 < nph:
                    slot_u[p + 2] = stream_weight(wu_src[p + 2])
                elif p == nph - 2 and not final:
                    nxt["wv"] = stream_weight(wvT)
            if not final:
                for mc in range(KC):
                    ln_sq(Y1B, bY1c, BR, bBRc, mc, dve=(mc < 4))

            if next_pro is not None:
                next_pro()


            if final:
                ln_reduce(Y1B, bY1c, B2, bB2c)
            else:
                ln_reduce(Y1B, bY1c, BR, bBRc)
            if next_r is not None:
                next_r()
                nxt["wr"] = stream_weight(wrT)
            def tail():
                gs = []

                def odma(c0, c1):
                    g = Buf("A2g")
                    g.w = bA2.w
                    go = Buf("OUTg")
                    P.op("sync", [go], [g],
                         lambda h=half_idx, a=c0, b=c1: SY.dma_start(
                             out_r[:, a:b, h * HALF:(h + 1) * HALF],
                             A2v[:, a:b, :]), dma=True)
                    gs.append(g)

                for mc in range(KC):
                    P.op("vector", [bA2], [bY1c[mc], bU[2]],
                         lambda m=mc: V.tensor_tensor(A2v[:, m, :], Y1v[:, m, :],
                                                      U[2][:, 0:HALF],
                                                      ALU.subtract))
                    P.op("vector", [bA2], [bA2, bU[3]],
                         lambda m=mc: V.tensor_tensor(A2v[:, m, :], A2v[:, m, :],
                                                      U[3][:, 0:HALF],
                                                      ALU.mult))
                    if mc == 3:
                        odma(0, 4)
                    elif mc == 5:
                        odma(4, 6)
                odma(6, 8)
                for g in gs:
                    for s, v in g.r.items():
                        if bA2.r.get(s, 0) < v:
                            bA2.r[s] = v
            return nxt, tail

        s_wk = stream_weight(wkT)
        s_wv = stream_weight(wvT)
        attn_phase(0, "warm", True, {"wk": s_wk, "wv": s_wv})
        s_wr = stream_weight(wrT)
        pre1 = attn_phase(WARM, "full", False,
                          {"wk": s_wk, "wv": s_wv, "wr": s_wr})
        nxt, tail1 = moe_phase(
            0,
            next_pro=lambda: attn_prologue(WARM + HALF, False, with_r=False),
            next_r=lambda: mixes([(BRv, bBRc, "tmr1")]),
            pre=pre1)
        tail1()
        pre2 = attn_phase(WARM + HALF, "full", False, nxt, prologue_done=True)
        _, tail2 = moe_phase(1, pre=pre2)
        tail2()

        P.barrier_wait("sync", [(s, P.count[s])
                                for s in ("dSP0", "dSP1", "dSP2")])
        P.barrier_wait("gpsimd", [(s, P.count[s])
                                  for s in ("dGP0", "dGP1", "dGP2",
                                            "dGP3", "dGP4", "dGP5")])
        P.barrier_wait("scalar", [("dACT0", P.count["dACT0"])])

        P.check_deadlock()

        blk = st.enter_context(nc.Block())

        @blk.tensor
        def _(eng):
            P.emit(eng, "tensor")

        @blk.vector
        def _(eng):
            P.emit(eng, "vector")

        @blk.scalar
        def _(eng):
            P.emit(eng, "scalar")

        @blk.gpsimd
        def _(eng):
            P.emit(eng, "gpsimd")

        @blk.sync
        def _(eng):
            P.emit(eng, "sync")

    return nc


_NC_CACHE = {}


def _get_nc():
    if "nc" not in _NC_CACHE:
        _NC_CACHE["nc"] = build_nc()
    return _NC_CACHE["nc"]


def kernel(L0_attn_x, L0_attn_time_mix_k, L0_attn_time_mix_v, L0_attn_time_mix_r,
           L0_attn_time_decay, L0_attn_time_first, L0_attn_Wk, L0_attn_Wv,
           L0_attn_Wr, L0_attn_Wo, L0_attn_ln_g, L0_attn_ln_b, L0_ffn_gate_w,
           L0_ffn_w_up, L0_ffn_w_down, L0_ffn_expert_bias):
    x = np.asarray(L0_attn_x, dtype=np.float32).reshape(NCORES * TC, C)

    def vec_layout(v):
        return np.ascontiguousarray(np.asarray(v, np.float32).reshape(KC, 128).T)

    vecs = np.concatenate([
        vec_layout(L0_attn_time_mix_k), vec_layout(L0_attn_time_mix_v),
        vec_layout(L0_attn_time_mix_r), vec_layout(L0_attn_time_decay),
        vec_layout(L0_attn_time_first), vec_layout(L0_attn_ln_g),
        vec_layout(L0_attn_ln_b)], axis=1).astype(np.float32)

    gate = np.asarray(L0_ffn_gate_w, np.float32)
    gateT = np.ascontiguousarray(
        gate.T.reshape(KC, 128, NE).transpose(1, 0, 2).reshape(128, KC * NE))

    cst = np.zeros((128, 408), np.float32)
    cst[:, 0:56] = vecs
    cst[:, 80:144] = gateT
    cst[:, 144:272] = np.eye(128, dtype=np.float32)
    cst[0:8, 272] = np.asarray(L0_ffn_expert_bias, np.float32).reshape(NE)
    cst[0, 273:401] = 1.0
    cstb = np.zeros((128, 1028), np.float32)
    cstb[:, 0] = 1.0
    for e in range(NE):
        cstb[e, 1 + e * 128:1 + (e + 1) * 128] = 1.0

    shared = {
        "wkT": np.ascontiguousarray(np.asarray(L0_attn_Wk, np.float32).T),
        "wvT": np.ascontiguousarray(np.asarray(L0_attn_Wv, np.float32).T),
        "wrT": np.ascontiguousarray(np.asarray(L0_attn_Wr, np.float32).T),
        "woT": np.ascontiguousarray(np.asarray(L0_attn_Wo, np.float32).T),
        "wuT": np.ascontiguousarray(
            np.asarray(L0_ffn_w_up, np.float32).transpose(0, 2, 1)),
        "wdT": np.ascontiguousarray(
            np.asarray(L0_ffn_w_down, np.float32).transpose(0, 2, 1)),
        "cst": cst,
        "cstb": cstb,
    }

    in_maps = []
    for c in range(NCORES):
        lo = c * TC - (WARM + 1)
        hi = (c + 1) * TC
        xs = np.zeros((WARM + 1 + TC, C), np.float32)
        a = max(lo, 0)
        xs[a - lo:hi - lo] = x[a:hi]
        m = dict(shared)
        m["xT"] = np.ascontiguousarray(xs.T)
        in_maps.append(m)

    nc = _get_nc()
    res = run_bass_kernel_spmd(nc, in_maps, core_ids=list(range(NCORES)))
    out = np.concatenate([np.asarray(r["out"]).T for r in res.results], axis=0)
    return out.reshape(1, NCORES * TC, C).astype(np.float32)


# version 43
# speedup vs baseline: 1.0098x; 1.0054x over previous
import contextlib
import numpy as np

import concourse.bass as bass
import concourse.mybir as mybir
from concourse.bass_utils import run_bass_kernel_spmd

F32 = mybir.dt.float32
BF16 = mybir.dt.bfloat16
AF = mybir.ActivationFunctionType
ALU = mybir.AluOpType
AX = mybir.AxisListType

NCORES = 8
C = 1024
KC = 8
TC = 2048
HALF = 1024
WARM = 512
NE = 8
HID = 2048
LN_EPS = 1e-5
NEG = -1e30
NT = HALF // 512

ENG = ("tensor", "vector", "scalar", "gpsimd", "sync")


class Buf:
    __slots__ = ("name", "w", "r")

    def __init__(self, name):
        self.name = name
        self.w = None
        self.r = {}


class Prog:

    def __init__(self, nc, sems):
        self.nc = nc
        self.sems = sems
        self.count = {k: 0 for k in sems}
        self.sat = {e: {} for e in ENG}
        self.ops = {e: [] for e in ENG}
        self.engsem = {"tensor": "ePE", "vector": "eDVE", "scalar": "eACT",
                       "gpsimd": "ePOOL", "sync": "eSP"}
        self.dmalanes = {"gpsimd": ["dGP0", "dGP1", "dGP2", "dGP3",
                                    "dGP4", "dGP5"],
                         "sync": ["dSP0", "dSP1", "dSP2"],
                         "scalar": ["dACT0"]}
        self.dmaidx = {k: 0 for k in self.dmalanes}

    def op(self, eng, outs, ins, fn, dma=False):
        waits = {}

        def need(m):
            if m is not None:
                s, v = m
                if v > waits.get(s, 0):
                    waits[s] = v

        for b in ins:
            need(b.w)
        for b in outs:
            need(b.w)
            for s, v in b.r.items():
                need((s, v))
        if dma:
            lanes = self.dmalanes[eng]
            cs = lanes[self.dmaidx[eng] % len(lanes)]
            self.dmaidx[eng] += 1
            inc = 16
            if self.count[cs] > 0:
                need((cs, self.count[cs]))
        else:
            cs = self.engsem[eng]
            inc = 1
        wl = []
        for s, v in sorted(waits.items()):
            if self.sat[eng].get(s, 0) >= v:
                continue
            if eng == "tensor" and s == "ePE":
                continue
            wl.append((s, v))
            self.sat[eng][s] = v
        self.count[cs] += inc
        v = self.count[cs]
        self.ops[eng].append((wl, fn, (cs, inc)))
        for b in outs:
            b.w = (cs, v)
            b.r = {}
        for b in ins:
            if b.r.get(cs, 0) < v:
                b.r[cs] = v
        return (cs, v)

    def barrier_wait(self, eng, milestones):
        wl = []
        for s, v in milestones:
            if self.sat[eng].get(s, 0) >= v:
                continue
            wl.append((s, v))
            self.sat[eng][s] = v
        if wl:
            self.ops[eng].append((wl, None, None))

    def check_deadlock(self):
        heads = {e: 0 for e in ENG}
        vals = {k: 0 for k in self.sems}
        total = sum(len(v) for v in self.ops.values())
        done = 0
        progress = True
        while progress:
            progress = False
            for e in ENG:
                while heads[e] < len(self.ops[e]):
                    wl, fn, incs = self.ops[e][heads[e]]
                    if any(vals[s] < v for s, v in wl):
                        break
                    if incs is not None:
                        vals[incs[0]] += incs[1]
                    heads[e] += 1
                    done += 1
                    progress = True
        if done != total:
            detail = []
            for e in ENG:
                if heads[e] < len(self.ops[e]):
                    wl = self.ops[e][heads[e]][0]
                    blocked = [(s, v, vals[s]) for s, v in wl if vals[s] < v]
                    detail.append(f"{e}@{heads[e]}/{len(self.ops[e])} waits {blocked}")
            raise RuntimeError("deadlock in emitted program: " + "; ".join(detail))

    def emit(self, eng_obj, eng):
        for wl, fn, incs in self.ops[eng]:
            for s, v in wl:
                eng_obj.wait_ge(self.sems[s], v)
            if fn is not None:
                inst = fn()
                if incs is not None:
                    inst.then_inc(self.sems[incs[0]], incs[1])


def col_bcast(t, ci, n, free):
    return bass.AP(t, ci, [[free, 128], [0, n]])


def build_nc():
    nc = bass.Bass("TRN2", target_bir_lowering=False)

    xT = nc.dram_tensor("xT", [C, WARM + 1 + TC], F32, kind="ExternalInput").ap()
    wkT = nc.dram_tensor("wkT", [C, C], F32, kind="ExternalInput").ap()
    wvT = nc.dram_tensor("wvT", [C, C], F32, kind="ExternalInput").ap()
    wrT = nc.dram_tensor("wrT", [C, C], F32, kind="ExternalInput").ap()
    woT = nc.dram_tensor("woT", [C, C], F32, kind="ExternalInput").ap()
    wuT = nc.dram_tensor("wuT", [NE, C, HID], F32, kind="ExternalInput").ap()
    wdT = nc.dram_tensor("wdT", [NE, HID, C], F32, kind="ExternalInput").ap()
    cst_d = nc.dram_tensor("cst", [128, 408], F32, kind="ExternalInput").ap()
    cstb_d = nc.dram_tensor("cstb", [128, 1028], F32, kind="ExternalInput").ap()
    out_d = nc.dram_tensor("out", [C, TC], F32, kind="ExternalOutput").ap()

    xT_r = xT.rearrange("(kc p) t -> p kc t", p=128)
    out_r = out_d.rearrange("(kc p) t -> p kc t", p=128)

    st = contextlib.ExitStack()
    with st:
        def sb(name, shape, dt):
            return st.enter_context(nc.sbuf_tensor(name, shape, dt))

        X = sb("X", [128, KC * HALF], F32)
        A2 = sb("A2", [128, KC * HALF], F32)
        D1 = sb("D1", [128, HALF], F32)
        DK0 = sb("DK0", [128, HALF], F32)
        DK1 = sb("DK1", [128, HALF], F32)
        B1 = sb("B1", [128, KC * HALF], BF16)
        B2 = sb("B2", [128, KC * HALF], BF16)
        BR = sb("BR", [128, KC * HALF], BF16)
        KV0 = sb("KV0", [128, HALF + 8], BF16)
        KV1 = sb("KV1", [128, HALF + 8], BF16)
        Y1B = sb("Y1B", [128, KC * HALF], BF16)
        W0 = sb("W0", [128, KC * 1024], BF16)
        W1 = sb("W1", [128, KC * 1024], BF16)
        W2 = sb("W2", [128, KC * 1024], BF16)
        WBt = sb("WBt", [128, HALF], BF16)
        TR0 = sb("TR0", [128, 1024], BF16)
        TR1 = sb("TR1", [128, 1024], BF16)
        CONST = sb("CONST", [128, 408], F32)
        CONSTB = sb("CONSTB", [128, 1028], BF16)
        VEC = CONST[:, 0:80]
        DEC = sb("DEC", [128, KC], F32)
        GT = CONST[:, 80:144]
        EB = CONST[0:8, 272:273]
        IDT = CONST[:, 144:272]
        OHB = CONSTB[0:8, 1:1025]
        ONR = CONST[0:1, 273:401]
        ONCB = CONSTB[:, 0:1]
        STATE = sb("STATE", [128, KC], F32)
        SSUM, SSSQ, STMP = D1, DK0, DK1
        EPSC = sb("EPSC", [1, 1], F32)
        NHC = sb("NHC", [1, 1], F32)
        MR1 = sb("MR1", [128, 8], F32)
        TB2 = sb("TB2", [128, 8], F32)
        TB3 = sb("TB3", [128, 8], F32)

        U = [st.enter_context(nc.psum_tensor(f"U{i}", [128, 1024], F32))
             for i in range(4)]

        sems = {}
        for s in ("ePE", "eDVE", "eACT", "ePOOL", "eSP",
                  "dGP0", "dGP1", "dGP2", "dGP3", "dGP4", "dGP5",
                  "dSP0", "dSP1", "dSP2", "dACT0"):
            sems[s] = st.enter_context(nc.semaphore(s))

        P = Prog(nc, sems)

        bA2, bD1 = Buf("A2"), Buf("D1")
        bXc = [Buf(f"X.{i}") for i in range(KC)]
        bDK = [Buf("DK0"), Buf("DK1")]
        bB1c = [Buf(f"B1.{i}") for i in range(KC)]
        bB2c = [Buf(f"B2.{i}") for i in range(KC)]
        bBRc = [Buf(f"BR.{i}") for i in range(KC)]
        bKV = [Buf("KV0"), Buf("KV1")]
        bY1c = [Buf(f"Y1B.{i}") for i in range(KC)]
        bWB = Buf("WB")
        bW = [Buf("W0"), Buf("W1"), Buf("W2")]
        bTR = [Buf("TR0"), Buf("TR1")]
        bCONST, bDEC, bCONSTB = Buf("CONST"), Buf("DEC"), Buf("CONSTB")
        bVEC = bGT = bEB = bIDT = bONR = bCONST
        bOHB = bONCB = bCONSTB
        bSTATE = Buf("STATE")
        bSSUM, bSSSQ, bSTMP = bD1, bDK[0], bDK[1]
        bMR1, bTB2, bTB3 = Buf("MR1"), Buf("TB2"), Buf("TB3")
        bU = [Buf(f"U{i}") for i in range(4)]
        bOUT = Buf("OUT")
        WSLOT = [W0, W1, W2]
        TRS = [TR0, TR1]
        DKS = [DK0, DK1]
        KVS = [KV0, KV1]

        V, S, T, G, SY = nc.vector, nc.scalar, nc.tensor, nc.gpsimd, nc.sync

        Xv = X[:, :].rearrange("p (c t) -> p c t", c=KC)
        A2v = A2[:, :].rearrange("p (c t) -> p c t", c=KC)
        B1v = B1[:, :].rearrange("p (c t) -> p c t", c=KC)
        B2v = B2[:, :].rearrange("p (c t) -> p c t", c=KC)
        BRv = BR[:, :].rearrange("p (c t) -> p c t", c=KC)
        Y1v = Y1B[:, :].rearrange("p (c t) -> p c t", c=KC)
        Wv_ = [w[:, :].rearrange("p (c m) -> p c m", c=KC) for w in WSLOT]

        vslice = {name: VEC[:, i * KC:(i + 1) * KC]
                  for i, name in enumerate(
                      ["tmk", "tmv", "tmr", "td", "tf", "lng", "lnb",
                       "tmk1", "tmv1", "tmr1"])}

        P.op("sync", [bCONST], [],
             lambda: SY.dma_start(CONST[:, :], cst_d), dma=True)
        P.op("gpsimd", [bCONSTB], [],
             lambda: G.dma_start(CONSTB[:, :], cstb_d), dma=True)
        P.op("scalar", [bDEC], [bVEC],
             lambda: S.activation(DEC[:, :], vslice["td"], AF.Exp))
        P.op("vector", [bMR1], [], lambda: V.memset(MR1[:, :], NEG))

        bEPSC = Buf("EPSC")
        P.op("vector", [bEPSC], [], lambda: V.memset(EPSC[:, :], LN_EPS))
        bNHC = Buf("NHC")
        P.op("vector", [bNHC], [], lambda: V.memset(NHC[:, :], -0.5))
        for a, b_ in (("tmk1", "tmk"), ("tmv1", "tmv"), ("tmr1", "tmr")):
            P.op("vector", [bVEC], [bVEC],
                 lambda a2=a, b2=b_: V.tensor_single_scalar(
                     vslice[a2], vslice[b2], 1.0, ALU.subtract))

        ring = {"i": 0}

        def stream_weight(src_ap):
            i = ring["i"] % 3
            ring["i"] += 1
            slot, bslot = WSLOT[i], bW[i]
            src = src_ap.rearrange("(kc p) m -> p kc m", p=128)
            P.op("gpsimd", [bslot], [],
                 lambda sl=slot, sr=src: G.dma_start(
                     sl[:, :].rearrange("p (c m) -> p c m", c=KC), sr), dma=True)
            return i

        def mm(ui, nt, lhsT, rhs, lhs_buf, rhs_buf, start, stop):
            M = lhsT.shape[-1]
            P.op("tensor", [bU[ui]], [lhs_buf, rhs_buf],
                 lambda u=ui, n=nt, l=lhsT, r=rhs, m_=M, st_=start, sp=stop:
                 T.matmul(U[u][0:m_, n * 512:n * 512 + r.shape[-1]], l, r,
                          start=st_, stop=sp))

        def proj_mms(ws, rhs_view, rhs_bufs, mc, pair, w=HALF):
            for nt in range(w // 512):
                for kc in range(KC):
                    mm(pair, nt, Wv_[ws][:, kc, mc * 128:(mc + 1) * 128],
                       rhs_view[:, kc, nt * 512:(nt + 1) * 512],
                       bW[ws], rhs_bufs[kc], start=(kc == 0), stop=(kc == KC - 1))

        def ln_copy(src_view, src_buf, bft, bfbs, mc, act=False):
            bfv = bft[:, :].rearrange("p (c t) -> p c t", c=KC)
            if act:
                P.op("scalar", [bfbs[mc]], [src_buf],
                     lambda m=mc, sv=src_view, bv=bfv: S.activation(
                         bv[:, m, :], sv[:, m, 0:HALF], AF.Copy))
            else:
                P.op("vector", [bfbs[mc]], [src_buf],
                     lambda m=mc, sv=src_view, bv=bfv: V.tensor_copy(
                         bv[:, m, :], sv[:, m, 0:HALF]))

        def ln_sq(bft, bfbs, sqt, sqbs, mc, dve=False):
            bfv = bft[:, :].rearrange("p (c t) -> p c t", c=KC)
            sqv = sqt[:, :].rearrange("p (c t) -> p c t", c=KC)
            if dve:
                P.op("vector", [sqbs[mc]], [bfbs[mc]],
                     lambda m=mc, bv=bfv, sv2=sqv: V.tensor_tensor(
                         sv2[:, m, :], bv[:, m, :], bv[:, m, :], ALU.mult))
            else:
                P.op("scalar", [sqbs[mc]], [bfbs[mc]],
                     lambda m=mc, bv=bfv, sv2=sqv: S.activation(
                         sv2[:, m, :], bv[:, m, :], AF.Square))

        def ln_copy_sq(src_view, src_buf, bft, bfbs, sqt, sqbs, mc):
            ln_copy(src_view, src_buf, bft, bfbs, mc)
            ln_sq(bft, bfbs, sqt, sqbs, mc)

        def ln_reduce(bft, bfbs, sqt, sqbs):
            bfv = bft[:, :].rearrange("p (c t) -> p c t", c=KC)
            sqv = sqt[:, :].rearrange("p (c t) -> p c t", c=KC)
            for nt in range(NT):
                for kc in range(KC):
                    mm(2, nt, ONCB[:, 0:1], bfv[:, kc, nt * 512:(nt + 1) * 512],
                       bONCB, bfbs[kc], start=(kc == 0), stop=(kc == KC - 1))
                for kc in range(KC):
                    mm(3, nt, ONCB[:, 0:1], sqv[:, kc, nt * 512:(nt + 1) * 512],
                       bONCB, sqbs[kc], start=(kc == 0), stop=(kc == KC - 1))
            P.op("vector", [bSSUM], [bU[2]],
                 lambda: V.tensor_copy(SSUM[0:1, :], U[2][0:1, 0:HALF]))
            P.op("vector", [bSSSQ], [bU[3]],
                 lambda: V.tensor_copy(SSSQ[0:1, :], U[3][0:1, 0:HALF]))
            inv = 1.0 / C
            P.op("vector", [bSSUM], [bSSUM],
                 lambda: V.tensor_single_scalar(SSUM[0:1, :], SSUM[0:1, :], inv,
                                                ALU.mult))
            P.op("vector", [bSTMP], [bSSUM],
                 lambda: V.tensor_mul(STMP[0:1, :], SSUM[0:1, :], SSUM[0:1, :]))
            P.op("vector", [bSSSQ], [bSSSQ, bSTMP],
                 lambda: V.scalar_tensor_tensor(SSSQ[0:1, :], SSSQ[0:1, :], inv,
                                                STMP[0:1, :], ALU.mult,
                                                ALU.subtract))
            P.op("scalar", [bSTMP], [bSSSQ, bEPSC],
                 lambda: S.activation(STMP[0:1, :], SSSQ[0:1, :], AF.Ln,
                                      bias=EPSC[0:1, 0:1]))
            P.op("scalar", [bSTMP], [bSTMP, bNHC],
                 lambda: S.activation(STMP[0:1, :], STMP[0:1, :], AF.Exp,
                                      scale=NHC[0:1, 0:1]))
            for nt in range(NT):
                mm(2, nt, ONR[0:1, :], SSUM[0:1, nt * 512:(nt + 1) * 512],
                   bONR, bSSUM, start=True, stop=True)
                mm(3, nt, ONR[0:1, :], STMP[0:1, nt * 512:(nt + 1) * 512],
                   bONR, bSTMP, start=True, stop=True)

        mix_cfg = {"c0": 0}

        def mixes(targets, w=HALF):
            c0 = mix_cfg["c0"]
            for ci in range(KC):
                pool = ci % 2 == 1
                d, bd = (DK0, bDK[0]) if pool else (D1, bD1)
                xs, bxs = KVS[ci % 2], bKV[ci % 2]
                P.op("gpsimd", [bxs], [],
                     lambda c=ci, x2=xs, w2=w, cc=c0: G.dma_start(
                         x2[:, 0:w2 + 1], xT_r[:, c, cc:cc + 1 + w2]), dma=True)
                P.op("vector", [bd], [bxs],
                     lambda d2=d, x2=xs, w2=w: V.tensor_sub(
                         d2[:, 0:w2], x2[:, 1:w2 + 1], x2[:, 0:w2]))
                for dv, dbufs, tm in targets:
                    P.op("vector", [dbufs[ci]], [bd, bxs, bVEC],
                         lambda c=ci, d2=d, x2=xs, dv2=dv, t_=tm, w2=w:
                         V.scalar_tensor_tensor(
                             dv2[:, c, 0:w2], d2[:, 0:w2], vslice[t_][:, c:c + 1],
                             x2[:, 1:w2 + 1], ALU.mult, ALU.add))

        def attn_prologue(x_col0, first, w=HALF, with_r=True):
            mix_cfg["c0"] = x_col0
            tg = [(B1v, bB1c, "tmk1"), (B2v, bB2c, "tmv1")]
            if with_r:
                tg.append((BRv, bBRc, "tmr1"))
            mixes(tg, w)

        def attn_phase(x_col0, kind, first, slots, prologue_done=False):
            warm = (kind == "warm")
            if not prologue_done:
                attn_prologue(x_col0, first, w=WARM if warm else HALF,
                              with_r=not warm)
            if not warm:
                for g2 in range(2):
                    P.op("sync", [bXc[g2 * 4 + i] for i in range(4)], [],
                         lambda c0=x_col0, g3=g2: SY.dma_start(
                             Xv[:, g3 * 4:(g3 + 1) * 4, :],
                             xT_r[:, g3 * 4:(g3 + 1) * 4,
                                  c0 + 1:c0 + 1 + HALF]), dma=True)

            w = WARM if warm else HALF
            for mc in range(KC):
                dk, bdk = DKS[mc % 2], bDK[mc % 2]
                kvt, bkv = KVS[mc % 2], bKV[mc % 2]
                uk = (mc % 2) * 2 if warm else 0
                uv = uk + 1
                proj_mms(slots["wk"], B1v, bB1c, mc, pair=uk, w=w)
                proj_mms(slots["wv"], B2v, bB2c, mc, pair=uv, w=w)
                tr, btr = TRS[mc % 2], bTR[mc % 2]
                P.op("scalar", [bdk], [bU[uk]],
                     lambda d=dk, w2=w, u=uk: S.activation(
                         d[:, 0:w2], U[u][:, 0:w2], AF.Copy))
                P.op("scalar", [btr], [bU[uv]],
                     lambda t2=tr, w2=w, u=uv: S.activation(
                         t2[:, 0:w2], U[u][:, 0:w2], AF.Copy))
                P.op("gpsimd", [bkv], [btr, bdk],
                     lambda d=dk, t2=tr, kv=kvt, w2=w: G.tensor_tensor(
                         kv[:, 0:w2], t2[:, 0:w2], d[:, 0:w2], ALU.mult))
                init = 0.0 if (warm and first) else STATE[:, mc:mc + 1]
                ins_l = [bkv, bDEC] + ([] if (warm and first) else [bSTATE])
                P.op("vector", [bA2], ins_l,
                     lambda m=mc, kv=kvt, i=init, w2=w: V.tensor_tensor_scan(
                         A2v[:, m, 0:w2], col_bcast(DEC, m, w2, KC), kv[:, 0:w2],
                         i, ALU.mult, ALU.add))
                P.op("vector", [bSTATE], [bA2],
                     lambda m=mc, w2=w: V.tensor_copy(
                         STATE[:, m:m + 1],
                         A2[:, m * HALF + w2 - 1:m * HALF + w2]))
                if not warm:
                    P.op("vector", [bA2], [bA2, bkv, bVEC],
                         lambda m=mc, kv=kvt: V.scalar_tensor_tensor(
                             A2v[:, m, :], kv[:, 0:HALF], vslice["tf"][:, m:m + 1],
                             A2v[:, m, :], ALU.mult, ALU.add))
            if warm:
                return
            s_wo = stream_weight(woT)
            pre = {"u0": stream_weight(wuT[0, :, 0:1024])}

            for mc in range(KC):
                pr = mc % 2
                proj_mms(slots["wr"], BRv, bBRc, mc, pair=pr)
                P.op("vector", [bB1c[mc]], [bU[pr]],
                     lambda m=mc, p_=pr: V.tensor_copy(B1v[:, m, :],
                                                       U[p_][:, 0:HALF]))
                P.op("vector", [bA2], [bA2, bB1c[mc]],
                     lambda m=mc: V.tensor_tensor(A2v[:, m, :], A2v[:, m, :],
                                                  B1v[:, m, :], ALU.mult))
                ln_copy(A2v, bA2, B2, bB2c, mc, act=True)
            pre["d0"] = stream_weight(wdT[0, 0:1024, :])
            for mc in range(KC):
                ln_sq(B2, bB2c, BR, bBRc, mc, dve=(mc < 4))
            ln_reduce(B2, bB2c, BR, bBRc)

            P.op("vector", [bTR[0]], [bU[2]],
                 lambda: V.tensor_copy(TR0[:, :], U[2][:, 0:HALF]))
            P.op("vector", [bTR[1]], [bU[3]],
                 lambda: V.tensor_copy(TR1[:, :], U[3][:, 0:HALF]))
            for mc in range(KC):
                P.op("vector", [bB2c[mc]], [bB2c[mc], bTR[0]],
                     lambda m=mc: V.tensor_tensor(B2v[:, m, :], B2v[:, m, :],
                                                  TR0[:, :], ALU.subtract))
                P.op("vector", [bB2c[mc]], [bB2c[mc], bTR[1]],
                     lambda m=mc: V.tensor_tensor(B2v[:, m, :], B2v[:, m, :],
                                                  TR1[:, :], ALU.mult))
                P.op("vector", [bB1c[mc]], [bB2c[mc], bVEC],
                     lambda m=mc: V.tensor_scalar(
                         B1v[:, m, :], B2v[:, m, :], vslice["lng"][:, m:m + 1],
                         vslice["lnb"][:, m:m + 1], ALU.mult, ALU.add))
            for mc in range(KC):
                pr = mc % 2
                proj_mms(s_wo, B1v, bB1c, mc, pair=pr)
                P.op("vector", [bA2], [bU[pr], bXc[mc]],
                     lambda m=mc, p_=pr: V.tensor_tensor(
                         A2v[:, m, :], U[p_][:, 0:HALF], Xv[:, m, :],
                         ALU.add))
                ln_copy(A2v, bA2, B2, bB2c, mc, act=True)
                ln_sq(B2, bB2c, BR, bBRc, mc, dve=True)
            pre["u1"] = stream_weight(wuT[0, :, 1024:2048])
            ln_reduce(B2, bB2c, BR, bBRc)

            P.op("vector", [bD1], [bU[2]],
                 lambda: V.tensor_copy(D1[:, :], U[2][:, 0:HALF]))
            P.op("vector", [bDK[0]], [bU[3]],
                 lambda: V.tensor_copy(DK0[:, :], U[3][:, 0:HALF]))
            for mc in range(KC):
                P.op("vector", [bXc[mc]], [bA2, bD1],
                     lambda m=mc: V.tensor_tensor(
                         Xv[:, m, 0:HALF], A2v[:, m, :], D1[:, :],
                         ALU.subtract))
                P.op("vector", [bXc[mc]], [bXc[mc], bDK[0]],
                     lambda m=mc: V.tensor_tensor(
                         Xv[:, m, 0:HALF], Xv[:, m, 0:HALF], DK0[:, :],
                         ALU.mult))
                P.op("scalar", [bY1c[mc]], [bXc[mc]],
                     lambda m=mc: S.activation(Y1v[:, m, :], Xv[:, m, 0:HALF],
                                               AF.Copy))
            return pre

        def moe_phase(half_idx, next_pro=None, next_r=None, pre=None):
            final = next_pro is None
            accv = Xv
            scores = A2[0:8, 0:HALF]
            biased = A2[0:8, HALF:2 * HALF]
            wselEM = A2[0:8, 2 * HALF:3 * HALF]
            stt = A2[:, 4 * HALF:4 * HALF + 64]
            wsel = A2[:, 4 * HALF + 128:4 * HALF + 192]

            for nt in range(NT):
                for kc in range(KC):
                    P.op("tensor", [bU[2]], [bGT, bXc[kc]],
                         lambda n=nt, k=kc: T.matmul(
                             U[2][0:8, n * 512:(n + 1) * 512],
                             GT[:, k * 8:(k + 1) * 8],
                             accv[:, k, n * 512:(n + 1) * 512],
                             start=(k == 0), stop=(k == KC - 1)))
            P.op("scalar", [bA2], [bU[2]],
                 lambda: S.activation(scores, U[2][0:8, 0:HALF], AF.Sigmoid))
            P.op("vector", [bA2], [bA2, bEB],
                 lambda: V.tensor_single_scalar(biased, scores, EB[:, 0:1],
                                                ALU.add))
            nsl = HALF // 128
            for s_ in range(nsl):
                P.op("tensor", [bU[3]], [bA2, bIDT],
                     lambda s2=s_: T.transpose(
                         U[3][:, s2 * 8:s2 * 8 + 8],
                         biased[:, s2 * 128:(s2 + 1) * 128], IDT[0:8, 0:8]))
            P.op("vector", [bA2], [bU[3]],
                 lambda: V.tensor_copy(stt, U[3][:, 0:8 * nsl]))
            for s_ in range(nsl):
                stB = stt[:, s_ * 8:s_ * 8 + 8]
                P.op("vector", [bMR1], [bA2, bMR1],
                     lambda sb_=stB: V.reduce_max(MR1[:, 0:1], sb_, AX.X))
                P.op("vector", [bTB2], [bMR1, bA2],
                     lambda sb_=stB: V.match_replace(TB2[:, :], MR1[:, 0:8], sb_,
                                                     NEG))
                P.op("vector", [bMR1], [bTB2, bMR1],
                     lambda: V.reduce_max(MR1[:, 0:1], TB2[:, :], AX.X))
                P.op("vector", [bTB3], [bMR1, bTB2],
                     lambda: V.match_replace(TB3[:, :], MR1[:, 0:8], TB2[:, :],
                                             NEG))
                P.op("vector", [bA2], [bA2, bTB3],
                     lambda s2=s_, sb_=stB: V.tensor_tensor(
                         wsel[:, s2 * 8:(s2 + 1) * 8], sb_, TB3[:, :],
                         ALU.not_equal))
                P.op("tensor", [bU[2]], [bA2, bIDT],
                     lambda s2=s_: T.transpose(
                         U[2][0:8, s2 * 128:(s2 + 1) * 128],
                         wsel[:, s2 * 8:(s2 + 1) * 8], IDT[:, :]))
            P.op("vector", [bA2], [bU[2]],
                 lambda: V.tensor_copy(wselEM, U[2][0:8, 0:HALF]))
            P.op("vector", [bKV[0]], [bA2],
                 lambda: V.tensor_mul(KV0[0:8, 0:HALF], wselEM, scores))

            def emit_wb_ye(e):
                for nt in range(NT):
                    P.op("tensor", [bU[2]], [bOHB, bKV[0]],
                         lambda n=nt, e2=e: T.matmul(
                             U[2][:, n * 512:(n + 1) * 512],
                             OHB[:, e2 * 128:(e2 + 1) * 128],
                             KV0[0:8, n * 512:(n + 1) * 512],
                             start=True, stop=True))
                P.op("vector", [bWB], [bU[2]],
                     lambda: V.tensor_copy(WBt[:, :], U[2][:, 0:HALF]))
                for ci in range(KC):
                    if e == 0:
                        P.op("vector", [bB1c[ci]], [bY1c[ci], bWB],
                             lambda c=ci: V.tensor_tensor(
                                 B1v[:, c, :], Y1v[:, c, :], WBt[:, :],
                                 ALU.mult))
                    else:
                        P.op("gpsimd", [bB1c[ci]], [bY1c[ci], bWB],
                             lambda c=ci: G.tensor_tensor(
                                 B1v[:, c, :], Y1v[:, c, :], WBt[:, :],
                                 ALU.mult))

            nph = NE * 2
            wu_src = [wuT[e, :, hh * 1024:(hh + 1) * 1024]
                      for e in range(NE) for hh in range(2)]
            wd_src = [wdT[e, hh * 1024:(hh + 1) * 1024, :]
                      for e in range(NE) for hh in range(2)]
            slot_u = [None] * nph
            slot_d = [None] * nph
            if pre is not None:
                slot_u[0], slot_d[0], slot_u[1] = (pre["u0"], pre["d0"],
                                                   pre["u1"])
            else:
                slot_u[0] = stream_weight(wu_src[0])
                slot_d[0] = stream_weight(wd_src[0])
                slot_u[1] = stream_weight(wu_src[1])
            emit_wb_ye(0)
            nxt = {}
            for p in range(nph):
                e, hh = divmod(p, 2)
                bhc = bB2c if hh == 0 else bBRc
                hv = B2v if hh == 0 else BRv
                wu_i, wd_i = slot_u[p], slot_d[p]
                for mc in range(KC):
                    pr = mc % 2
                    for nt in range(NT):
                        for kc in range(KC):
                            mm(pr, nt, Wv_[wu_i][:, kc, mc * 128:(mc + 1) * 128],
                               B1v[:, kc, nt * 512:(nt + 1) * 512], bW[wu_i],
                               bB1c[kc], start=(kc == 0), stop=(kc == KC - 1))
                    tr, btr = TRS[mc % 2], bTR[mc % 2]
                    P.op("scalar", [btr], [bU[pr]],
                         lambda p_=pr, t2=tr: S.activation(t2[:, :],
                                                           U[p_][:, 0:HALF],
                                                           AF.Relu))
                    P.op("vector", [bhc[mc]], [btr],
                         lambda m=mc, t2=tr, h2=hv: V.tensor_tensor(
                             h2[:, m, :], t2[:, :], t2[:, :], ALU.mult))
                if hh == 1 and e + 1 < NE:
                    emit_wb_ye(e + 1)
                if p + 1 < nph:
                    slot_d[p + 1] = stream_weight(wd_src[p + 1])
                elif not final:
                    nxt["wk"] = stream_weight(wkT)
                for cc in range(KC):
                    pr = cc % 2
                    for nt in range(NT):
                        for kc in range(KC):
                            mm(pr, nt, Wv_[wd_i][:, kc, cc * 128:(cc + 1) * 128],
                               hv[:, kc, nt * 512:(nt + 1) * 512], bW[wd_i],
                               bhc[kc], start=(kc == 0), stop=(kc == KC - 1))
                    P.op("vector", [bXc[cc]], [bU[pr], bXc[cc]],
                         lambda c2=cc, p_=pr: V.tensor_tensor(
                             Xv[:, c2, 0:HALF], U[p_][:, 0:HALF],
                             Xv[:, c2, 0:HALF], ALU.add))
                    if p == nph - 1:
                        P.op("scalar", [bY1c[cc]], [bXc[cc]],
                             lambda m=cc: S.activation(Y1v[:, m, :],
                                                       Xv[:, m, 0:HALF],
                                                       AF.Copy))
                        if final:
                            ln_sq(Y1B, bY1c, B2, bB2c, cc, dve=False)
                if p + 2 < nph:
                    slot_u[p + 2] = stream_weight(wu_src[p + 2])
                elif p == nph - 2 and not final:
                    nxt["wv"] = stream_weight(wvT)
            if not final:
                for mc in range(KC):
                    ln_sq(Y1B, bY1c, BR, bBRc, mc, dve=(mc < 4))

            if next_pro is not None:
                next_pro()


            if final:
                ln_reduce(Y1B, bY1c, B2, bB2c)
            else:
                ln_reduce(Y1B, bY1c, BR, bBRc)
            if next_r is not None:
                next_r()
                nxt["wr"] = stream_weight(wrT)
            def tail():
                gs = []

                def odma(c0, c1):
                    g = Buf("A2g")
                    g.w = bA2.w
                    go = Buf("OUTg")
                    P.op("sync", [go], [g],
                         lambda h=half_idx, a=c0, b=c1: SY.dma_start(
                             out_r[:, a:b, h * HALF:(h + 1) * HALF],
                             A2v[:, a:b, :]), dma=True)
                    gs.append(g)

                for mc in range(KC):
                    P.op("vector", [bA2], [bY1c[mc], bU[2]],
                         lambda m=mc: V.tensor_tensor(A2v[:, m, :], Y1v[:, m, :],
                                                      U[2][:, 0:HALF],
                                                      ALU.subtract))
                    P.op("vector", [bA2], [bA2, bU[3]],
                         lambda m=mc: V.tensor_tensor(A2v[:, m, :], A2v[:, m, :],
                                                      U[3][:, 0:HALF],
                                                      ALU.mult))
                    if mc == 3:
                        odma(0, 4)
                    elif mc == 5:
                        odma(4, 6)
                odma(6, 8)
                for g in gs:
                    for s, v in g.r.items():
                        if bA2.r.get(s, 0) < v:
                            bA2.r[s] = v
            return nxt, tail

        s_wk = stream_weight(wkT)
        s_wv = stream_weight(wvT)
        attn_phase(0, "warm", True, {"wk": s_wk, "wv": s_wv})
        s_wr = stream_weight(wrT)
        pre1 = attn_phase(WARM, "full", False,
                          {"wk": s_wk, "wv": s_wv, "wr": s_wr})
        nxt, tail1 = moe_phase(
            0,
            next_pro=lambda: attn_prologue(WARM + HALF, False, with_r=False),
            next_r=lambda: mixes([(BRv, bBRc, "tmr1")]),
            pre=pre1)
        tail1()
        pre2 = attn_phase(WARM + HALF, "full", False, nxt, prologue_done=True)
        _, tail2 = moe_phase(1, pre=pre2)
        tail2()

        P.barrier_wait("sync", [(s, P.count[s])
                                for s in ("dSP0", "dSP1", "dSP2")])
        P.barrier_wait("gpsimd", [(s, P.count[s])
                                  for s in ("dGP0", "dGP1", "dGP2",
                                            "dGP3", "dGP4", "dGP5")])
        P.barrier_wait("scalar", [("dACT0", P.count["dACT0"])])

        P.check_deadlock()

        blk = st.enter_context(nc.Block())

        @blk.tensor
        def _(eng):
            P.emit(eng, "tensor")

        @blk.vector
        def _(eng):
            P.emit(eng, "vector")

        @blk.scalar
        def _(eng):
            P.emit(eng, "scalar")

        @blk.gpsimd
        def _(eng):
            P.emit(eng, "gpsimd")

        @blk.sync
        def _(eng):
            P.emit(eng, "sync")

    return nc


_NC_CACHE = {}


def _get_nc():
    if "nc" not in _NC_CACHE:
        _NC_CACHE["nc"] = build_nc()
    return _NC_CACHE["nc"]


def kernel(L0_attn_x, L0_attn_time_mix_k, L0_attn_time_mix_v, L0_attn_time_mix_r,
           L0_attn_time_decay, L0_attn_time_first, L0_attn_Wk, L0_attn_Wv,
           L0_attn_Wr, L0_attn_Wo, L0_attn_ln_g, L0_attn_ln_b, L0_ffn_gate_w,
           L0_ffn_w_up, L0_ffn_w_down, L0_ffn_expert_bias):
    x = np.asarray(L0_attn_x, dtype=np.float32).reshape(NCORES * TC, C)

    def vec_layout(v):
        return np.ascontiguousarray(np.asarray(v, np.float32).reshape(KC, 128).T)

    vecs = np.concatenate([
        vec_layout(L0_attn_time_mix_k), vec_layout(L0_attn_time_mix_v),
        vec_layout(L0_attn_time_mix_r), vec_layout(L0_attn_time_decay),
        vec_layout(L0_attn_time_first), vec_layout(L0_attn_ln_g),
        vec_layout(L0_attn_ln_b)], axis=1).astype(np.float32)

    gate = np.asarray(L0_ffn_gate_w, np.float32)
    gateT = np.ascontiguousarray(
        gate.T.reshape(KC, 128, NE).transpose(1, 0, 2).reshape(128, KC * NE))

    cst = np.zeros((128, 408), np.float32)
    cst[:, 0:56] = vecs
    cst[:, 80:144] = gateT
    cst[:, 144:272] = np.eye(128, dtype=np.float32)
    cst[0:8, 272] = np.asarray(L0_ffn_expert_bias, np.float32).reshape(NE)
    cst[0, 273:401] = 1.0
    cstb = np.zeros((128, 1028), np.float32)
    cstb[:, 0] = 1.0
    for e in range(NE):
        cstb[e, 1 + e * 128:1 + (e + 1) * 128] = 1.0

    shared = {
        "wkT": np.ascontiguousarray(np.asarray(L0_attn_Wk, np.float32).T),
        "wvT": np.ascontiguousarray(np.asarray(L0_attn_Wv, np.float32).T),
        "wrT": np.ascontiguousarray(np.asarray(L0_attn_Wr, np.float32).T),
        "woT": np.ascontiguousarray(np.asarray(L0_attn_Wo, np.float32).T),
        "wuT": np.ascontiguousarray(
            np.asarray(L0_ffn_w_up, np.float32).transpose(0, 2, 1)),
        "wdT": np.ascontiguousarray(
            np.asarray(L0_ffn_w_down, np.float32).transpose(0, 2, 1)),
        "cst": cst,
        "cstb": cstb,
    }

    in_maps = []
    for c in range(NCORES):
        lo = c * TC - (WARM + 1)
        hi = (c + 1) * TC
        xs = np.zeros((WARM + 1 + TC, C), np.float32)
        a = max(lo, 0)
        xs[a - lo:hi - lo] = x[a:hi]
        m = dict(shared)
        m["xT"] = np.ascontiguousarray(xs.T)
        in_maps.append(m)

    nc = _get_nc()
    res = run_bass_kernel_spmd(nc, in_maps, core_ids=list(range(NCORES)))
    out = np.concatenate([np.asarray(r["out"]).T for r in res.results], axis=0)
    return out.reshape(1, NCORES * TC, C).astype(np.float32)


# version 44 (final)
# speedup vs baseline: 1.0109x; 1.0010x over previous
import contextlib
import numpy as np

import concourse.bass as bass
import concourse.mybir as mybir
from concourse.bass_utils import run_bass_kernel_spmd

F32 = mybir.dt.float32
BF16 = mybir.dt.bfloat16
AF = mybir.ActivationFunctionType
ALU = mybir.AluOpType
AX = mybir.AxisListType

NCORES = 8
C = 1024
KC = 8
TC = 2048
HALF = 1024
WARM = 512
NE = 8
HID = 2048
LN_EPS = 1e-5
NEG = -1e30
NT = HALF // 512

ENG = ("tensor", "vector", "scalar", "gpsimd", "sync")


class Buf:
    __slots__ = ("name", "w", "r")

    def __init__(self, name):
        self.name = name
        self.w = None
        self.r = {}


class Prog:

    def __init__(self, nc, sems):
        self.nc = nc
        self.sems = sems
        self.count = {k: 0 for k in sems}
        self.sat = {e: {} for e in ENG}
        self.ops = {e: [] for e in ENG}
        self.engsem = {"tensor": "ePE", "vector": "eDVE", "scalar": "eACT",
                       "gpsimd": "ePOOL", "sync": "eSP"}
        self.dmalanes = {"gpsimd": ["dGP0", "dGP1", "dGP2", "dGP3",
                                    "dGP4", "dGP5"],
                         "sync": ["dSP0", "dSP1", "dSP2"],
                         "scalar": ["dACT0"]}
        self.dmaidx = {k: 0 for k in self.dmalanes}

    def op(self, eng, outs, ins, fn, dma=False):
        waits = {}

        def need(m):
            if m is not None:
                s, v = m
                if v > waits.get(s, 0):
                    waits[s] = v

        for b in ins:
            need(b.w)
        for b in outs:
            need(b.w)
            for s, v in b.r.items():
                need((s, v))
        if dma:
            lanes = self.dmalanes[eng]
            cs = lanes[self.dmaidx[eng] % len(lanes)]
            self.dmaidx[eng] += 1
            inc = 16
            if self.count[cs] > 0:
                need((cs, self.count[cs]))
        else:
            cs = self.engsem[eng]
            inc = 1
        wl = []
        for s, v in sorted(waits.items()):
            if self.sat[eng].get(s, 0) >= v:
                continue
            if eng == "tensor" and s == "ePE":
                continue
            wl.append((s, v))
            self.sat[eng][s] = v
        self.count[cs] += inc
        v = self.count[cs]
        self.ops[eng].append((wl, fn, (cs, inc)))
        for b in outs:
            b.w = (cs, v)
            b.r = {}
        for b in ins:
            if b.r.get(cs, 0) < v:
                b.r[cs] = v
        return (cs, v)

    def barrier_wait(self, eng, milestones):
        wl = []
        for s, v in milestones:
            if self.sat[eng].get(s, 0) >= v:
                continue
            wl.append((s, v))
            self.sat[eng][s] = v
        if wl:
            self.ops[eng].append((wl, None, None))

    def check_deadlock(self):
        heads = {e: 0 for e in ENG}
        vals = {k: 0 for k in self.sems}
        total = sum(len(v) for v in self.ops.values())
        done = 0
        progress = True
        while progress:
            progress = False
            for e in ENG:
                while heads[e] < len(self.ops[e]):
                    wl, fn, incs = self.ops[e][heads[e]]
                    if any(vals[s] < v for s, v in wl):
                        break
                    if incs is not None:
                        vals[incs[0]] += incs[1]
                    heads[e] += 1
                    done += 1
                    progress = True
        if done != total:
            detail = []
            for e in ENG:
                if heads[e] < len(self.ops[e]):
                    wl = self.ops[e][heads[e]][0]
                    blocked = [(s, v, vals[s]) for s, v in wl if vals[s] < v]
                    detail.append(f"{e}@{heads[e]}/{len(self.ops[e])} waits {blocked}")
            raise RuntimeError("deadlock in emitted program: " + "; ".join(detail))

    def emit(self, eng_obj, eng):
        for wl, fn, incs in self.ops[eng]:
            for s, v in wl:
                eng_obj.wait_ge(self.sems[s], v)
            if fn is not None:
                inst = fn()
                if incs is not None:
                    inst.then_inc(self.sems[incs[0]], incs[1])


def col_bcast(t, ci, n, free):
    return bass.AP(t, ci, [[free, 128], [0, n]])


def build_nc():
    nc = bass.Bass("TRN2", target_bir_lowering=False)

    xT = nc.dram_tensor("xT", [C, WARM + 1 + TC], F32, kind="ExternalInput").ap()
    wkT = nc.dram_tensor("wkT", [C, C], F32, kind="ExternalInput").ap()
    wvT = nc.dram_tensor("wvT", [C, C], F32, kind="ExternalInput").ap()
    wrT = nc.dram_tensor("wrT", [C, C], F32, kind="ExternalInput").ap()
    woT = nc.dram_tensor("woT", [C, C], F32, kind="ExternalInput").ap()
    wuT = nc.dram_tensor("wuT", [NE, C, HID], F32, kind="ExternalInput").ap()
    wdT = nc.dram_tensor("wdT", [NE, HID, C], F32, kind="ExternalInput").ap()
    cst_d = nc.dram_tensor("cst", [128, 408], F32, kind="ExternalInput").ap()
    cstb_d = nc.dram_tensor("cstb", [128, 1028], F32, kind="ExternalInput").ap()
    out_d = nc.dram_tensor("out", [C, TC], F32, kind="ExternalOutput").ap()

    xT_r = xT.rearrange("(kc p) t -> p kc t", p=128)
    out_r = out_d.rearrange("(kc p) t -> p kc t", p=128)

    st = contextlib.ExitStack()
    with st:
        def sb(name, shape, dt):
            return st.enter_context(nc.sbuf_tensor(name, shape, dt))

        X = sb("X", [128, KC * HALF], F32)
        A2 = sb("A2", [128, KC * HALF], F32)
        D1 = sb("D1", [128, HALF], F32)
        DK0 = sb("DK0", [128, HALF], F32)
        DK1 = sb("DK1", [128, HALF], F32)
        B1 = sb("B1", [128, KC * HALF], BF16)
        B2 = sb("B2", [128, KC * HALF], BF16)
        BR = sb("BR", [128, KC * HALF], BF16)
        KV0 = sb("KV0", [128, HALF + 8], BF16)
        KV1 = sb("KV1", [128, HALF + 8], BF16)
        Y1B = sb("Y1B", [128, KC * HALF], BF16)
        W0 = sb("W0", [128, KC * 1024], BF16)
        W1 = sb("W1", [128, KC * 1024], BF16)
        W2 = sb("W2", [128, KC * 1024], BF16)
        WBt = sb("WBt", [128, HALF], BF16)
        TR0 = sb("TR0", [128, 1024], BF16)
        TR1 = sb("TR1", [128, 1024], BF16)
        CONST = sb("CONST", [128, 408], F32)
        CONSTB = sb("CONSTB", [128, 1028], BF16)
        VEC = CONST[:, 0:80]
        DEC = sb("DEC", [128, KC], F32)
        GT = CONST[:, 80:144]
        EB = CONST[0:8, 272:273]
        IDT = CONST[:, 144:272]
        OHB = CONSTB[0:8, 1:1025]
        ONR = CONST[0:1, 273:401]
        ONCB = CONSTB[:, 0:1]
        STATE = sb("STATE", [128, KC], F32)
        SSUM, SSSQ, STMP = D1, DK0, DK1
        EPSC = sb("EPSC", [1, 1], F32)
        NHC = sb("NHC", [1, 1], F32)
        MR1 = sb("MR1", [128, 8], F32)
        TB2 = sb("TB2", [128, 8], F32)
        TB3 = sb("TB3", [128, 8], F32)

        U = [st.enter_context(nc.psum_tensor(f"U{i}", [128, 1024], F32))
             for i in range(4)]

        sems = {}
        for s in ("ePE", "eDVE", "eACT", "ePOOL", "eSP",
                  "dGP0", "dGP1", "dGP2", "dGP3", "dGP4", "dGP5",
                  "dSP0", "dSP1", "dSP2", "dACT0"):
            sems[s] = st.enter_context(nc.semaphore(s))

        P = Prog(nc, sems)

        bA2, bD1 = Buf("A2"), Buf("D1")
        bXc = [Buf(f"X.{i}") for i in range(KC)]
        bDK = [Buf("DK0"), Buf("DK1")]
        bB1c = [Buf(f"B1.{i}") for i in range(KC)]
        bB2c = [Buf(f"B2.{i}") for i in range(KC)]
        bBRc = [Buf(f"BR.{i}") for i in range(KC)]
        bKV = [Buf("KV0"), Buf("KV1")]
        bY1c = [Buf(f"Y1B.{i}") for i in range(KC)]
        bWB = Buf("WB")
        bW = [Buf("W0"), Buf("W1"), Buf("W2")]
        bTR = [Buf("TR0"), Buf("TR1")]
        bCONST, bDEC, bCONSTB = Buf("CONST"), Buf("DEC"), Buf("CONSTB")
        bVEC = bGT = bEB = bIDT = bONR = bCONST
        bOHB = bONCB = bCONSTB
        bSTATE = Buf("STATE")
        bSSUM, bSSSQ, bSTMP = bD1, bDK[0], bDK[1]
        bMR1, bTB2, bTB3 = Buf("MR1"), Buf("TB2"), Buf("TB3")
        bU = [Buf(f"U{i}") for i in range(4)]
        bOUT = Buf("OUT")
        WSLOT = [W0, W1, W2]
        TRS = [TR0, TR1]
        DKS = [DK0, DK1]
        KVS = [KV0, KV1]

        V, S, T, G, SY = nc.vector, nc.scalar, nc.tensor, nc.gpsimd, nc.sync

        Xv = X[:, :].rearrange("p (c t) -> p c t", c=KC)
        A2v = A2[:, :].rearrange("p (c t) -> p c t", c=KC)
        B1v = B1[:, :].rearrange("p (c t) -> p c t", c=KC)
        B2v = B2[:, :].rearrange("p (c t) -> p c t", c=KC)
        BRv = BR[:, :].rearrange("p (c t) -> p c t", c=KC)
        Y1v = Y1B[:, :].rearrange("p (c t) -> p c t", c=KC)
        Wv_ = [w[:, :].rearrange("p (c m) -> p c m", c=KC) for w in WSLOT]

        vslice = {name: VEC[:, i * KC:(i + 1) * KC]
                  for i, name in enumerate(
                      ["tmk", "tmv", "tmr", "td", "tf", "lng", "lnb",
                       "tmk1", "tmv1", "tmr1"])}

        P.op("sync", [bCONST], [],
             lambda: SY.dma_start(CONST[:, :], cst_d), dma=True)
        P.op("gpsimd", [bCONSTB], [],
             lambda: G.dma_start(CONSTB[:, :], cstb_d), dma=True)
        P.op("scalar", [bDEC], [bVEC],
             lambda: S.activation(DEC[:, :], vslice["td"], AF.Exp))
        P.op("vector", [bMR1], [], lambda: V.memset(MR1[:, :], NEG))

        bEPSC = Buf("EPSC")
        P.op("vector", [bEPSC], [], lambda: V.memset(EPSC[:, :], LN_EPS))
        bNHC = Buf("NHC")
        P.op("vector", [bNHC], [], lambda: V.memset(NHC[:, :], -0.5))
        for a, b_ in (("tmk1", "tmk"), ("tmv1", "tmv"), ("tmr1", "tmr")):
            P.op("vector", [bVEC], [bVEC],
                 lambda a2=a, b2=b_: V.tensor_single_scalar(
                     vslice[a2], vslice[b2], 1.0, ALU.subtract))

        ring = {"i": 0}

        def stream_weight(src_ap):
            i = ring["i"] % 3
            ring["i"] += 1
            slot, bslot = WSLOT[i], bW[i]
            src = src_ap.rearrange("(kc p) m -> p kc m", p=128)
            P.op("gpsimd", [bslot], [],
                 lambda sl=slot, sr=src: G.dma_start(
                     sl[:, :].rearrange("p (c m) -> p c m", c=KC), sr), dma=True)
            return i

        def mm(ui, nt, lhsT, rhs, lhs_buf, rhs_buf, start, stop):
            M = lhsT.shape[-1]
            P.op("tensor", [bU[ui]], [lhs_buf, rhs_buf],
                 lambda u=ui, n=nt, l=lhsT, r=rhs, m_=M, st_=start, sp=stop:
                 T.matmul(U[u][0:m_, n * 512:n * 512 + r.shape[-1]], l, r,
                          start=st_, stop=sp))

        def proj_mms(ws, rhs_view, rhs_bufs, mc, pair, w=HALF):
            for nt in range(w // 512):
                for kc in range(KC):
                    mm(pair, nt, Wv_[ws][:, kc, mc * 128:(mc + 1) * 128],
                       rhs_view[:, kc, nt * 512:(nt + 1) * 512],
                       bW[ws], rhs_bufs[kc], start=(kc == 0), stop=(kc == KC - 1))

        def ln_copy(src_view, src_buf, bft, bfbs, mc, act=False):
            bfv = bft[:, :].rearrange("p (c t) -> p c t", c=KC)
            if act:
                P.op("scalar", [bfbs[mc]], [src_buf],
                     lambda m=mc, sv=src_view, bv=bfv: S.activation(
                         bv[:, m, :], sv[:, m, 0:HALF], AF.Copy))
            else:
                P.op("vector", [bfbs[mc]], [src_buf],
                     lambda m=mc, sv=src_view, bv=bfv: V.tensor_copy(
                         bv[:, m, :], sv[:, m, 0:HALF]))

        def ln_sq(bft, bfbs, sqt, sqbs, mc, dve=False):
            bfv = bft[:, :].rearrange("p (c t) -> p c t", c=KC)
            sqv = sqt[:, :].rearrange("p (c t) -> p c t", c=KC)
            if dve:
                P.op("vector", [sqbs[mc]], [bfbs[mc]],
                     lambda m=mc, bv=bfv, sv2=sqv: V.tensor_tensor(
                         sv2[:, m, :], bv[:, m, :], bv[:, m, :], ALU.mult))
            else:
                P.op("scalar", [sqbs[mc]], [bfbs[mc]],
                     lambda m=mc, bv=bfv, sv2=sqv: S.activation(
                         sv2[:, m, :], bv[:, m, :], AF.Square))

        def ln_copy_sq(src_view, src_buf, bft, bfbs, sqt, sqbs, mc):
            ln_copy(src_view, src_buf, bft, bfbs, mc)
            ln_sq(bft, bfbs, sqt, sqbs, mc)

        def ln_reduce(bft, bfbs, sqt, sqbs):
            bfv = bft[:, :].rearrange("p (c t) -> p c t", c=KC)
            sqv = sqt[:, :].rearrange("p (c t) -> p c t", c=KC)
            for nt in range(NT):
                for kc in range(KC):
                    mm(2, nt, ONCB[:, 0:1], bfv[:, kc, nt * 512:(nt + 1) * 512],
                       bONCB, bfbs[kc], start=(kc == 0), stop=(kc == KC - 1))
                for kc in range(KC):
                    mm(3, nt, ONCB[:, 0:1], sqv[:, kc, nt * 512:(nt + 1) * 512],
                       bONCB, sqbs[kc], start=(kc == 0), stop=(kc == KC - 1))
            P.op("vector", [bSSUM], [bU[2]],
                 lambda: V.tensor_copy(SSUM[0:1, :], U[2][0:1, 0:HALF]))
            P.op("vector", [bSSSQ], [bU[3]],
                 lambda: V.tensor_copy(SSSQ[0:1, :], U[3][0:1, 0:HALF]))
            inv = 1.0 / C
            P.op("vector", [bSSUM], [bSSUM],
                 lambda: V.tensor_single_scalar(SSUM[0:1, :], SSUM[0:1, :], inv,
                                                ALU.mult))
            for nt in range(NT):
                mm(2, nt, ONR[0:1, :], SSUM[0:1, nt * 512:(nt + 1) * 512],
                   bONR, bSSUM, start=True, stop=True)
            P.op("vector", [bSTMP], [bSSUM],
                 lambda: V.tensor_mul(STMP[0:1, :], SSUM[0:1, :], SSUM[0:1, :]))
            P.op("vector", [bSSSQ], [bSSSQ, bSTMP],
                 lambda: V.scalar_tensor_tensor(SSSQ[0:1, :], SSSQ[0:1, :], inv,
                                                STMP[0:1, :], ALU.mult,
                                                ALU.subtract))
            P.op("scalar", [bSTMP], [bSSSQ, bEPSC],
                 lambda: S.activation(STMP[0:1, :], SSSQ[0:1, :], AF.Ln,
                                      bias=EPSC[0:1, 0:1]))
            P.op("scalar", [bSTMP], [bSTMP, bNHC],
                 lambda: S.activation(STMP[0:1, :], STMP[0:1, :], AF.Exp,
                                      scale=NHC[0:1, 0:1]))
            for nt in range(NT):
                mm(3, nt, ONR[0:1, :], STMP[0:1, nt * 512:(nt + 1) * 512],
                   bONR, bSTMP, start=True, stop=True)

        mix_cfg = {"c0": 0}

        def mixes(targets, w=HALF):
            c0 = mix_cfg["c0"]
            for ci in range(KC):
                pool = ci % 2 == 1
                d, bd = (DK0, bDK[0]) if pool else (D1, bD1)
                xs, bxs = KVS[ci % 2], bKV[ci % 2]
                P.op("gpsimd", [bxs], [],
                     lambda c=ci, x2=xs, w2=w, cc=c0: G.dma_start(
                         x2[:, 0:w2 + 1], xT_r[:, c, cc:cc + 1 + w2]), dma=True)
                P.op("vector", [bd], [bxs],
                     lambda d2=d, x2=xs, w2=w: V.tensor_sub(
                         d2[:, 0:w2], x2[:, 1:w2 + 1], x2[:, 0:w2]))
                for dv, dbufs, tm in targets:
                    P.op("vector", [dbufs[ci]], [bd, bxs, bVEC],
                         lambda c=ci, d2=d, x2=xs, dv2=dv, t_=tm, w2=w:
                         V.scalar_tensor_tensor(
                             dv2[:, c, 0:w2], d2[:, 0:w2], vslice[t_][:, c:c + 1],
                             x2[:, 1:w2 + 1], ALU.mult, ALU.add))

        def attn_prologue(x_col0, first, w=HALF, with_r=True):
            mix_cfg["c0"] = x_col0
            tg = [(B1v, bB1c, "tmk1"), (B2v, bB2c, "tmv1")]
            if with_r:
                tg.append((BRv, bBRc, "tmr1"))
            mixes(tg, w)

        def attn_phase(x_col0, kind, first, slots, prologue_done=False):
            warm = (kind == "warm")
            if not prologue_done:
                attn_prologue(x_col0, first, w=WARM if warm else HALF,
                              with_r=not warm)
            if not warm:
                for g2 in range(2):
                    P.op("sync", [bXc[g2 * 4 + i] for i in range(4)], [],
                         lambda c0=x_col0, g3=g2: SY.dma_start(
                             Xv[:, g3 * 4:(g3 + 1) * 4, :],
                             xT_r[:, g3 * 4:(g3 + 1) * 4,
                                  c0 + 1:c0 + 1 + HALF]), dma=True)

            w = WARM if warm else HALF
            for mc in range(KC):
                dk, bdk = DKS[mc % 2], bDK[mc % 2]
                kvt, bkv = KVS[mc % 2], bKV[mc % 2]
                uk = (mc % 2) * 2 if warm else 0
                uv = uk + 1
                proj_mms(slots["wk"], B1v, bB1c, mc, pair=uk, w=w)
                proj_mms(slots["wv"], B2v, bB2c, mc, pair=uv, w=w)
                tr, btr = TRS[mc % 2], bTR[mc % 2]
                P.op("scalar", [bdk], [bU[uk]],
                     lambda d=dk, w2=w, u=uk: S.activation(
                         d[:, 0:w2], U[u][:, 0:w2], AF.Copy))
                P.op("scalar", [btr], [bU[uv]],
                     lambda t2=tr, w2=w, u=uv: S.activation(
                         t2[:, 0:w2], U[u][:, 0:w2], AF.Copy))
                P.op("gpsimd", [bkv], [btr, bdk],
                     lambda d=dk, t2=tr, kv=kvt, w2=w: G.tensor_tensor(
                         kv[:, 0:w2], t2[:, 0:w2], d[:, 0:w2], ALU.mult))
                init = 0.0 if (warm and first) else STATE[:, mc:mc + 1]
                ins_l = [bkv, bDEC] + ([] if (warm and first) else [bSTATE])
                P.op("vector", [bA2], ins_l,
                     lambda m=mc, kv=kvt, i=init, w2=w: V.tensor_tensor_scan(
                         A2v[:, m, 0:w2], col_bcast(DEC, m, w2, KC), kv[:, 0:w2],
                         i, ALU.mult, ALU.add))
                P.op("vector", [bSTATE], [bA2],
                     lambda m=mc, w2=w: V.tensor_copy(
                         STATE[:, m:m + 1],
                         A2[:, m * HALF + w2 - 1:m * HALF + w2]))
                if not warm:
                    P.op("vector", [bA2], [bA2, bkv, bVEC],
                         lambda m=mc, kv=kvt: V.scalar_tensor_tensor(
                             A2v[:, m, :], kv[:, 0:HALF], vslice["tf"][:, m:m + 1],
                             A2v[:, m, :], ALU.mult, ALU.add))
            if warm:
                return
            s_wo = stream_weight(woT)
            pre = {"u0": stream_weight(wuT[0, :, 0:1024])}

            for mc in range(KC):
                pr = mc % 2
                proj_mms(slots["wr"], BRv, bBRc, mc, pair=pr)
                P.op("vector", [bB1c[mc]], [bU[pr]],
                     lambda m=mc, p_=pr: V.tensor_copy(B1v[:, m, :],
                                                       U[p_][:, 0:HALF]))
                P.op("vector", [bA2], [bA2, bB1c[mc]],
                     lambda m=mc: V.tensor_tensor(A2v[:, m, :], A2v[:, m, :],
                                                  B1v[:, m, :], ALU.mult))
                ln_copy(A2v, bA2, B2, bB2c, mc, act=True)
            pre["d0"] = stream_weight(wdT[0, 0:1024, :])
            for mc in range(KC):
                ln_sq(B2, bB2c, BR, bBRc, mc, dve=(mc < 4))
            ln_reduce(B2, bB2c, BR, bBRc)

            P.op("vector", [bTR[0]], [bU[2]],
                 lambda: V.tensor_copy(TR0[:, :], U[2][:, 0:HALF]))
            P.op("vector", [bTR[1]], [bU[3]],
                 lambda: V.tensor_copy(TR1[:, :], U[3][:, 0:HALF]))
            for mc in range(KC):
                P.op("vector", [bB2c[mc]], [bB2c[mc], bTR[0]],
                     lambda m=mc: V.tensor_tensor(B2v[:, m, :], B2v[:, m, :],
                                                  TR0[:, :], ALU.subtract))
                P.op("vector", [bB2c[mc]], [bB2c[mc], bTR[1]],
                     lambda m=mc: V.tensor_tensor(B2v[:, m, :], B2v[:, m, :],
                                                  TR1[:, :], ALU.mult))
                P.op("vector", [bB1c[mc]], [bB2c[mc], bVEC],
                     lambda m=mc: V.tensor_scalar(
                         B1v[:, m, :], B2v[:, m, :], vslice["lng"][:, m:m + 1],
                         vslice["lnb"][:, m:m + 1], ALU.mult, ALU.add))
            for mc in range(KC):
                pr = mc % 2
                proj_mms(s_wo, B1v, bB1c, mc, pair=pr)
                P.op("vector", [bA2], [bU[pr], bXc[mc]],
                     lambda m=mc, p_=pr: V.tensor_tensor(
                         A2v[:, m, :], U[p_][:, 0:HALF], Xv[:, m, :],
                         ALU.add))
                ln_copy(A2v, bA2, B2, bB2c, mc, act=True)
                ln_sq(B2, bB2c, BR, bBRc, mc, dve=True)
            pre["u1"] = stream_weight(wuT[0, :, 1024:2048])
            ln_reduce(B2, bB2c, BR, bBRc)

            P.op("vector", [bD1], [bU[2]],
                 lambda: V.tensor_copy(D1[:, :], U[2][:, 0:HALF]))
            P.op("vector", [bDK[0]], [bU[3]],
                 lambda: V.tensor_copy(DK0[:, :], U[3][:, 0:HALF]))
            for mc in range(KC):
                P.op("vector", [bXc[mc]], [bA2, bD1],
                     lambda m=mc: V.tensor_tensor(
                         Xv[:, m, 0:HALF], A2v[:, m, :], D1[:, :],
                         ALU.subtract))
                P.op("vector", [bXc[mc]], [bXc[mc], bDK[0]],
                     lambda m=mc: V.tensor_tensor(
                         Xv[:, m, 0:HALF], Xv[:, m, 0:HALF], DK0[:, :],
                         ALU.mult))
                P.op("scalar", [bY1c[mc]], [bXc[mc]],
                     lambda m=mc: S.activation(Y1v[:, m, :], Xv[:, m, 0:HALF],
                                               AF.Copy))
            return pre

        def moe_phase(half_idx, next_pro=None, next_r=None, pre=None):
            final = next_pro is None
            accv = Xv
            scores = A2[0:8, 0:HALF]
            biased = A2[0:8, HALF:2 * HALF]
            wselEM = A2[0:8, 2 * HALF:3 * HALF]
            stt = A2[:, 4 * HALF:4 * HALF + 64]
            wsel = A2[:, 4 * HALF + 128:4 * HALF + 192]

            for nt in range(NT):
                for kc in range(KC):
                    P.op("tensor", [bU[2]], [bGT, bXc[kc]],
                         lambda n=nt, k=kc: T.matmul(
                             U[2][0:8, n * 512:(n + 1) * 512],
                             GT[:, k * 8:(k + 1) * 8],
                             accv[:, k, n * 512:(n + 1) * 512],
                             start=(k == 0), stop=(k == KC - 1)))
            P.op("scalar", [bA2], [bU[2]],
                 lambda: S.activation(scores, U[2][0:8, 0:HALF], AF.Sigmoid))
            P.op("vector", [bA2], [bA2, bEB],
                 lambda: V.tensor_single_scalar(biased, scores, EB[:, 0:1],
                                                ALU.add))
            nsl = HALF // 128
            for s_ in range(nsl):
                P.op("tensor", [bU[3]], [bA2, bIDT],
                     lambda s2=s_: T.transpose(
                         U[3][:, s2 * 8:s2 * 8 + 8],
                         biased[:, s2 * 128:(s2 + 1) * 128], IDT[0:8, 0:8]))
            P.op("vector", [bA2], [bU[3]],
                 lambda: V.tensor_copy(stt, U[3][:, 0:8 * nsl]))
            for s_ in range(nsl):
                stB = stt[:, s_ * 8:s_ * 8 + 8]
                P.op("vector", [bMR1], [bA2, bMR1],
                     lambda sb_=stB: V.reduce_max(MR1[:, 0:1], sb_, AX.X))
                P.op("vector", [bTB2], [bMR1, bA2],
                     lambda sb_=stB: V.match_replace(TB2[:, :], MR1[:, 0:8], sb_,
                                                     NEG))
                P.op("vector", [bMR1], [bTB2, bMR1],
                     lambda: V.reduce_max(MR1[:, 0:1], TB2[:, :], AX.X))
                P.op("vector", [bTB3], [bMR1, bTB2],
                     lambda: V.match_replace(TB3[:, :], MR1[:, 0:8], TB2[:, :],
                                             NEG))
                P.op("vector", [bA2], [bA2, bTB3],
                     lambda s2=s_, sb_=stB: V.tensor_tensor(
                         wsel[:, s2 * 8:(s2 + 1) * 8], sb_, TB3[:, :],
                         ALU.not_equal))
                P.op("tensor", [bU[2]], [bA2, bIDT],
                     lambda s2=s_: T.transpose(
                         U[2][0:8, s2 * 128:(s2 + 1) * 128],
                         wsel[:, s2 * 8:(s2 + 1) * 8], IDT[:, :]))
            P.op("vector", [bA2], [bU[2]],
                 lambda: V.tensor_copy(wselEM, U[2][0:8, 0:HALF]))
            P.op("vector", [bKV[0]], [bA2],
                 lambda: V.tensor_mul(KV0[0:8, 0:HALF], wselEM, scores))

            def emit_wb_ye(e):
                for nt in range(NT):
                    P.op("tensor", [bU[2]], [bOHB, bKV[0]],
                         lambda n=nt, e2=e: T.matmul(
                             U[2][:, n * 512:(n + 1) * 512],
                             OHB[:, e2 * 128:(e2 + 1) * 128],
                             KV0[0:8, n * 512:(n + 1) * 512],
                             start=True, stop=True))
                P.op("vector", [bWB], [bU[2]],
                     lambda: V.tensor_copy(WBt[:, :], U[2][:, 0:HALF]))
                for ci in range(KC):
                    if e == 0:
                        P.op("vector", [bB1c[ci]], [bY1c[ci], bWB],
                             lambda c=ci: V.tensor_tensor(
                                 B1v[:, c, :], Y1v[:, c, :], WBt[:, :],
                                 ALU.mult))
                    else:
                        P.op("gpsimd", [bB1c[ci]], [bY1c[ci], bWB],
                             lambda c=ci: G.tensor_tensor(
                                 B1v[:, c, :], Y1v[:, c, :], WBt[:, :],
                                 ALU.mult))

            nph = NE * 2
            wu_src = [wuT[e, :, hh * 1024:(hh + 1) * 1024]
                      for e in range(NE) for hh in range(2)]
            wd_src = [wdT[e, hh * 1024:(hh + 1) * 1024, :]
                      for e in range(NE) for hh in range(2)]
            slot_u = [None] * nph
            slot_d = [None] * nph
            if pre is not None:
                slot_u[0], slot_d[0], slot_u[1] = (pre["u0"], pre["d0"],
                                                   pre["u1"])
            else:
                slot_u[0] = stream_weight(wu_src[0])
                slot_d[0] = stream_weight(wd_src[0])
                slot_u[1] = stream_weight(wu_src[1])
            emit_wb_ye(0)
            nxt = {}
            for p in range(nph):
                e, hh = divmod(p, 2)
                bhc = bB2c if hh == 0 else bBRc
                hv = B2v if hh == 0 else BRv
                wu_i, wd_i = slot_u[p], slot_d[p]
                for mc in range(KC):
                    pr = mc % 2
                    for nt in range(NT):
                        for kc in range(KC):
                            mm(pr, nt, Wv_[wu_i][:, kc, mc * 128:(mc + 1) * 128],
                               B1v[:, kc, nt * 512:(nt + 1) * 512], bW[wu_i],
                               bB1c[kc], start=(kc == 0), stop=(kc == KC - 1))
                    tr, btr = TRS[mc % 2], bTR[mc % 2]
                    P.op("scalar", [btr], [bU[pr]],
                         lambda p_=pr, t2=tr: S.activation(t2[:, :],
                                                           U[p_][:, 0:HALF],
                                                           AF.Relu))
                    P.op("vector", [bhc[mc]], [btr],
                         lambda m=mc, t2=tr, h2=hv: V.tensor_tensor(
                             h2[:, m, :], t2[:, :], t2[:, :], ALU.mult))
                if hh == 1 and e + 1 < NE:
                    emit_wb_ye(e + 1)
                if p + 1 < nph:
                    slot_d[p + 1] = stream_weight(wd_src[p + 1])
                elif not final:
                    nxt["wk"] = stream_weight(wkT)
                for cc in range(KC):
                    pr = cc % 2
                    for nt in range(NT):
                        for kc in range(KC):
                            mm(pr, nt, Wv_[wd_i][:, kc, cc * 128:(cc + 1) * 128],
                               hv[:, kc, nt * 512:(nt + 1) * 512], bW[wd_i],
                               bhc[kc], start=(kc == 0), stop=(kc == KC - 1))
                    P.op("vector", [bXc[cc]], [bU[pr], bXc[cc]],
                         lambda c2=cc, p_=pr: V.tensor_tensor(
                             Xv[:, c2, 0:HALF], U[p_][:, 0:HALF],
                             Xv[:, c2, 0:HALF], ALU.add))
                    if p == nph - 1:
                        P.op("scalar", [bY1c[cc]], [bXc[cc]],
                             lambda m=cc: S.activation(Y1v[:, m, :],
                                                       Xv[:, m, 0:HALF],
                                                       AF.Copy))
                        if final:
                            ln_sq(Y1B, bY1c, B2, bB2c, cc, dve=False)
                if p + 2 < nph:
                    slot_u[p + 2] = stream_weight(wu_src[p + 2])
                elif p == nph - 2 and not final:
                    nxt["wv"] = stream_weight(wvT)
            if not final:
                for mc in range(KC):
                    ln_sq(Y1B, bY1c, BR, bBRc, mc, dve=(mc < 4))

            if next_pro is not None:
                next_pro()


            if final:
                ln_reduce(Y1B, bY1c, B2, bB2c)
            else:
                ln_reduce(Y1B, bY1c, BR, bBRc)
            if next_r is not None:
                next_r()
                nxt["wr"] = stream_weight(wrT)
            def tail():
                gs = []

                def odma(c0, c1):
                    g = Buf("A2g")
                    g.w = bA2.w
                    go = Buf("OUTg")
                    P.op("sync", [go], [g],
                         lambda h=half_idx, a=c0, b=c1: SY.dma_start(
                             out_r[:, a:b, h * HALF:(h + 1) * HALF],
                             A2v[:, a:b, :]), dma=True)
                    gs.append(g)

                for mc in range(KC):
                    P.op("vector", [bA2], [bY1c[mc], bU[2]],
                         lambda m=mc: V.tensor_tensor(A2v[:, m, :], Y1v[:, m, :],
                                                      U[2][:, 0:HALF],
                                                      ALU.subtract))
                for mc in range(KC):
                    P.op("vector", [bA2], [bA2, bU[3]],
                         lambda m=mc: V.tensor_tensor(A2v[:, m, :], A2v[:, m, :],
                                                      U[3][:, 0:HALF],
                                                      ALU.mult))
                    if mc == 3:
                        odma(0, 4)
                    elif mc == 5:
                        odma(4, 6)
                odma(6, 8)
                for g in gs:
                    for s, v in g.r.items():
                        if bA2.r.get(s, 0) < v:
                            bA2.r[s] = v
            return nxt, tail

        s_wk = stream_weight(wkT)
        s_wv = stream_weight(wvT)
        attn_phase(0, "warm", True, {"wk": s_wk, "wv": s_wv})
        s_wr = stream_weight(wrT)
        pre1 = attn_phase(WARM, "full", False,
                          {"wk": s_wk, "wv": s_wv, "wr": s_wr})
        nxt, tail1 = moe_phase(
            0,
            next_pro=lambda: attn_prologue(WARM + HALF, False, with_r=False),
            next_r=lambda: mixes([(BRv, bBRc, "tmr1")]),
            pre=pre1)
        tail1()
        pre2 = attn_phase(WARM + HALF, "full", False, nxt, prologue_done=True)
        _, tail2 = moe_phase(1, pre=pre2)
        tail2()

        P.barrier_wait("sync", [(s, P.count[s])
                                for s in ("dSP0", "dSP1", "dSP2")])
        P.barrier_wait("gpsimd", [(s, P.count[s])
                                  for s in ("dGP0", "dGP1", "dGP2",
                                            "dGP3", "dGP4", "dGP5")])
        P.barrier_wait("scalar", [("dACT0", P.count["dACT0"])])

        P.check_deadlock()

        blk = st.enter_context(nc.Block())

        @blk.tensor
        def _(eng):
            P.emit(eng, "tensor")

        @blk.vector
        def _(eng):
            P.emit(eng, "vector")

        @blk.scalar
        def _(eng):
            P.emit(eng, "scalar")

        @blk.gpsimd
        def _(eng):
            P.emit(eng, "gpsimd")

        @blk.sync
        def _(eng):
            P.emit(eng, "sync")

    return nc


_NC_CACHE = {}


def _get_nc():
    if "nc" not in _NC_CACHE:
        _NC_CACHE["nc"] = build_nc()
    return _NC_CACHE["nc"]


def kernel(L0_attn_x, L0_attn_time_mix_k, L0_attn_time_mix_v, L0_attn_time_mix_r,
           L0_attn_time_decay, L0_attn_time_first, L0_attn_Wk, L0_attn_Wv,
           L0_attn_Wr, L0_attn_Wo, L0_attn_ln_g, L0_attn_ln_b, L0_ffn_gate_w,
           L0_ffn_w_up, L0_ffn_w_down, L0_ffn_expert_bias):
    x = np.asarray(L0_attn_x, dtype=np.float32).reshape(NCORES * TC, C)

    def vec_layout(v):
        return np.ascontiguousarray(np.asarray(v, np.float32).reshape(KC, 128).T)

    vecs = np.concatenate([
        vec_layout(L0_attn_time_mix_k), vec_layout(L0_attn_time_mix_v),
        vec_layout(L0_attn_time_mix_r), vec_layout(L0_attn_time_decay),
        vec_layout(L0_attn_time_first), vec_layout(L0_attn_ln_g),
        vec_layout(L0_attn_ln_b)], axis=1).astype(np.float32)

    gate = np.asarray(L0_ffn_gate_w, np.float32)
    gateT = np.ascontiguousarray(
        gate.T.reshape(KC, 128, NE).transpose(1, 0, 2).reshape(128, KC * NE))

    cst = np.zeros((128, 408), np.float32)
    cst[:, 0:56] = vecs
    cst[:, 80:144] = gateT
    cst[:, 144:272] = np.eye(128, dtype=np.float32)
    cst[0:8, 272] = np.asarray(L0_ffn_expert_bias, np.float32).reshape(NE)
    cst[0, 273:401] = 1.0
    cstb = np.zeros((128, 1028), np.float32)
    cstb[:, 0] = 1.0
    for e in range(NE):
        cstb[e, 1 + e * 128:1 + (e + 1) * 128] = 1.0

    shared = {
        "wkT": np.ascontiguousarray(np.asarray(L0_attn_Wk, np.float32).T),
        "wvT": np.ascontiguousarray(np.asarray(L0_attn_Wv, np.float32).T),
        "wrT": np.ascontiguousarray(np.asarray(L0_attn_Wr, np.float32).T),
        "woT": np.ascontiguousarray(np.asarray(L0_attn_Wo, np.float32).T),
        "wuT": np.ascontiguousarray(
            np.asarray(L0_ffn_w_up, np.float32).transpose(0, 2, 1)),
        "wdT": np.ascontiguousarray(
            np.asarray(L0_ffn_w_down, np.float32).transpose(0, 2, 1)),
        "cst": cst,
        "cstb": cstb,
    }

    in_maps = []
    for c in range(NCORES):
        lo = c * TC - (WARM + 1)
        hi = (c + 1) * TC
        xs = np.zeros((WARM + 1 + TC, C), np.float32)
        a = max(lo, 0)
        xs[a - lo:hi - lo] = x[a:hi]
        m = dict(shared)
        m["xT"] = np.ascontiguousarray(xs.T)
        in_maps.append(m)

    nc = _get_nc()
    res = run_bass_kernel_spmd(nc, in_maps, core_ids=list(range(NCORES)))
    out = np.concatenate([np.asarray(r["out"]).T for r in res.results], axis=0)
    return out.reshape(1, NCORES * TC, C).astype(np.float32)
